# version 11
# baseline (speedup 1.0000x reference)
import contextlib
import numpy as np
import ml_dtypes
import concourse.bass as bass
import concourse.mybir as mybir
from concourse.bass_utils import run_bass_kernel_spmd

F32 = mybir.dt.float32
BF16 = mybir.dt.bfloat16
AF = mybir.ActivationFunctionType
ALU = mybir.AluOpType
AX = mybir.AxisListType
BF = ml_dtypes.bfloat16

T, D, HID, NH, DK = 4096, 1024, 4096, 16, 64
NCH, CH, NTT = 8, 512, 32
LN_EPS = 1e-5
ABS_EPS = 1e-6
EXP_BIAS = -12.0
ENGS = ("sync", "tensor", "vector", "scalar", "gpsimd")


class Ctx:
    def __init__(self):
        self.counts = {}
        self.emit_counts = {}
        self.events = {}
        self.seq = {}


class Em:
    def __init__(self, ctx, sems, eng, dry, engname=None):
        self.ctx, self.sems, self.eng, self.dry = ctx, sems, eng, dry
        self.engname = engname

    def do(self, sem_name, fn, inc=1, event=None):
        if self.dry:
            c = self.ctx.counts.get(sem_name, 0) + inc
            self.ctx.counts[sem_name] = c
            if inc:
                self.ctx.seq[self.engname].append(("inc", sem_name, inc))
            if event is not None:
                assert event not in self.ctx.events, f"dup event {event}"
                self.ctx.events[event] = (sem_name, c)
        else:
            ins = fn(self.eng)
            if inc:
                ins.then_inc(self.sems[sem_name], inc)
            c = self.ctx.emit_counts.get(sem_name, 0) + inc
            self.ctx.emit_counts[sem_name] = c
            if event is not None:
                assert self.ctx.events[event] == (sem_name, c), f"event drift {event}"

    def wait(self, *events):
        if self.dry:
            for ev in events:
                if ev is not None:
                    self.ctx.seq[self.engname].append(("wait", ev))
            return
        need = {}
        for ev in events:
            if ev is None:
                continue
            s, v = self.ctx.events[ev]
            need[s] = max(need.get(s, 0), v)
        for s, v in need.items():
            self.eng.wait_ge(self.sems[s], v)

    def wait_cnt(self, sem_name, v):
        if self.dry:
            self.ctx.seq[self.engname].append(("waitc", sem_name, v))
        else:
            self.eng.wait_ge(self.sems[sem_name], v)


def host_inputs(full):
    x = np.ascontiguousarray(np.asarray(full["L0_attn_x"], dtype=np.float32))
    Wq = np.asarray(full["L0_attn_Wq"], dtype=np.float32)
    Wk = np.asarray(full["L0_attn_Wk"], dtype=np.float32)
    Wv = np.asarray(full["L0_attn_Wv"], dtype=np.float32)
    Wo = np.asarray(full["L0_attn_Wo"], dtype=np.float32)
    Wfc = np.asarray(full["L0_ffn_W_fc"], dtype=np.float32)
    Wpr = np.asarray(full["L0_ffn_W_proj"], dtype=np.float32)
    lnw = np.asarray(full["L0_ffn_ln_w"], dtype=np.float32)
    lnb = np.asarray(full["L0_ffn_ln_b"], dtype=np.float32)

    perm_idx = np.concatenate(
        [np.concatenate([h * 64 + 2 * np.arange(32), h * 64 + 2 * np.arange(32) + 1])
         for h in range(NH)])
    const = {
        "wqT": np.ascontiguousarray(Wq[perm_idx, :].T),
        "wkT": np.ascontiguousarray(Wk[perm_idx, :].T),
        "wvT": np.ascontiguousarray(Wv.T),
        "woT": np.ascontiguousarray(Wo.T),
        "wfcT": np.ascontiguousarray(Wfc.T),
        "wprT": np.ascontiguousarray(Wpr.T),
        "lnw_col": np.ascontiguousarray(lnw.reshape(32, 128).T),
        "lnb_col": np.ascontiguousarray(lnb.reshape(32, 128).T),
    }
    j = np.arange(32, dtype=np.float64)
    inv = 10000.0 ** (-2.0 * j / 64.0)
    ang = np.arange(T, dtype=np.float64)[None, :] * inv[:, None]
    cos32 = np.cos(ang).astype(np.float32)
    sin32 = np.sin(ang).astype(np.float32)
    const["tabC"] = np.ascontiguousarray(np.tile(cos32, (4, 1)))
    const["tabS"] = np.ascontiguousarray(
        np.concatenate([-sin32, sin32, -sin32, sin32], axis=0))
    swap = np.arange(128)
    swap = np.where((swap % 64) < 32, swap + 32, swap - 32)
    permM = np.zeros((128, 128), np.float32)
    permM[np.arange(128), swap] = 1.0
    const["permM"] = np.asarray(permM.astype(BF))
    const["tri"] = np.asarray(np.triu(np.ones((128, 128), np.float32)).astype(BF))
    const["idb"] = np.asarray(np.eye(128, dtype=np.float32).astype(BF))
    const["idf"] = np.eye(128, dtype=np.float32)
    return [dict(const, x=np.ascontiguousarray(x[c])) for c in range(8)]


def nkt(qc):
    return 4 * qc + 4


def _check_deadlock(ctx, start_counts, name):
    sem = dict(start_counts)
    pos = {en: 0 for en in ctx.seq}
    seqs = ctx.seq
    progress = True
    while progress:
        progress = False
        for en, s in seqs.items():
            while pos[en] < len(s):
                op = s[pos[en]]
                if op[0] == "inc":
                    sem[op[1]] = sem.get(op[1], 0) + op[2]
                elif op[0] == "wait":
                    sm, v = ctx.events[op[1]]
                    if sem.get(sm, 0) < v:
                        break
                else:
                    if sem.get(op[1], 0) < op[2]:
                        break
                pos[en] += 1
                progress = True
    stuck = {en: (pos[en], len(s)) for en, s in seqs.items() if pos[en] < len(s)}
    if stuck:
        detail = {}
        for en in stuck:
            op = seqs[en][pos[en]]
            if op[0] == "wait":
                sm, v = ctx.events[op[1]]
                detail[en] = f"waits {op[1]} ({sm}>={v}, have {dict(sem).get(sm, 0)})"
            else:
                detail[en] = f"waits {op[1]}>={op[2]} (have {dict(sem).get(op[1], 0)})"
        raise RuntimeError(f"DEADLOCK in phase {name}: {stuck} :: {detail}")


NQB = 4


def build(dbg=False):
    import os
    PHASES = int(os.environ.get("PHASES", "9"))
    P1GAMMA = os.environ.get("P1GAMMA", "1") == "1"
    P1QKV = os.environ.get("P1QKV", "1") == "1"
    P1ROPE = os.environ.get("P1ROPE", "1") == "1"
    P1V = os.environ.get("P1V", "1") == "1"
    P1PERM = os.environ.get("P1PERM", "1") == "1"
    nc = bass.Bass("TRN2", target_bir_lowering=False)
    io = {}
    io["x"] = nc.dram_tensor("x", [T, D], F32, kind="ExternalInput")
    for n, sh, dt in [
        ("wqT", [D, D], F32), ("wkT", [D, D], F32), ("wvT", [D, D], F32),
        ("woT", [D, D], F32), ("wfcT", [D, HID], F32), ("wprT", [HID, D], F32),
        ("lnw_col", [128, 32], F32), ("lnb_col", [128, 32], F32),
        ("tabC", [128, T], F32), ("tabS", [128, T], F32),
        ("permM", [128, 128], BF16), ("tri", [128, 128], BF16),
        ("idb", [128, 128], BF16), ("idf", [128, 128], F32),
    ]:
        io[n] = nc.dram_tensor(n, sh, dt, kind="ExternalInput")
    io["out"] = nc.dram_tensor("out", [T, D], F32, kind="ExternalOutput")
    spk = "ExternalOutput" if dbg else "Internal"
    io["xn_sp"] = nc.dram_tensor("xn_sp", [D, T], BF16, kind=spk)
    io["qT_sp"] = nc.dram_tensor("qT_sp", [D, T], BF16, kind=spk)
    io["kT_sp"] = nc.dram_tensor("kT_sp", [D, T], BF16, kind=spk)
    io["v_sp"] = nc.dram_tensor("v_sp", [T, NH * 65], BF16, kind=spk)
    io["g_sp"] = nc.dram_tensor("g_sp", [T, D], BF16, kind=spk)

    ctx = Ctx()
    sems = {}
    with contextlib.ExitStack() as stack:
        stack.enter_context(nc.allow_low_precision(
            reason="bf16 tensor_tensor ops are 0/1-mask multiplies (exact) or h^2 at bf16 (within tolerance)"))
        QSEMS = {
            "sync": ["dIn", "cLNB", "cPRM", "tab0", "tab1", "tab2", "xb0", "xb1", "xb2", "xb3", "xb4", "wl0", "wl1",
                      "g2l0", "g2l1", "xnl", "pr0", "pr1", "xg0", "xg1"],
            "scalar": ["dOut", "xnsp0", "xnsp1", "qsp0", "qsp1", "ksp0", "ksp1", "vsp0", "vsp1",
                        "gspS", "osp0", "osp1"],
            "gpsimd": ["dGp", "gl0", "gl1", "otb0", "otb1"],
        }
        for sn in (["PE", "ACT", "DVE", "GP", "bar"] + QSEMS["sync"] + QSEMS["scalar"]
                   + QSEMS["gpsimd"]):
            sems[sn] = stack.enter_context(nc.semaphore(sn))

        konst = stack.enter_context(nc.sbuf_tensor("konst", [128, 8], F32))
        ones_b = stack.enter_context(nc.sbuf_tensor("ones_b", [128, 128], BF16))
        ones_f = stack.enter_context(nc.sbuf_tensor("ones_f", [128, 128], F32))
        idb = stack.enter_context(nc.sbuf_tensor("idb_s", [128, 128], BF16))
        idf = stack.enter_context(nc.sbuf_tensor("idf_s", [128, 128], F32))
        lnwc = stack.enter_context(nc.sbuf_tensor("lnw_s", [128, 32], F32))
        lnbc = stack.enter_context(nc.sbuf_tensor("lnb_s", [128, 32], F32))
        lnbc_bf = stack.enter_context(nc.sbuf_tensor("lnbb_s", [128, 32], BF16))
        thr = stack.enter_context(nc.sbuf_tensor("thr", [1, 8], F32))
        thrB = stack.enter_context(nc.sbuf_tensor("thrB", [128, 4], F32))

        barv = [0]

        def run_phase(builders, post_dry=None, name=""):
            start_counts = dict(ctx.counts)
            ctx.seq = {en: [] for en in ENGS}
            for en in ENGS:
                b = builders.get(en)
                if b:
                    b(Em(ctx, sems, None, True, engname=en))
            if post_dry:
                post_dry()
            _check_deadlock(ctx, start_counts, name)
            prev_bar = barv[0]
            barv[0] += len(ENGS)
            with nc.Block() as block:
                def mk(en, b):
                    def f(eng):
                        E = Em(ctx, sems, eng, False)
                        if prev_bar:
                            E.wait_cnt("bar", prev_bar)
                        if b:
                            b(E)
                        for sn in QSEMS.get(en, ()):
                            if ctx.counts.get(sn, 0):
                                E.wait_cnt(sn, ctx.counts[sn])
                        eng.nop().then_inc(sems["bar"], 1)
                    return f
                for en in ENGS:
                    getattr(block, en)(mk(en, builders.get(en)))

        def p0_sync(E):
            for dst, src in [(idb, "idb"), (idf, "idf"), (lnwc, "lnw_col")]:
                E.do("dIn", lambda e, dst=dst, src=src: e.dma_start(out=dst[:, :], in_=io[src][:, :]),
                     16, event=f"ld_{src}")
            E.do("cLNB", lambda e: e.dma_start(out=lnbc[:, :], in_=io["lnb_col"][:, :]),
                 16, event="ld_lnb_col")

        def p0_vec(E):
            E.do("DVE", lambda e: e.memset(konst[:, 1:2], EXP_BIAS))
            E.do("DVE", lambda e: e.memset(ones_b[:, :], 1.0))
            E.do("DVE", lambda e: e.memset(ones_f[:, :], 1.0))
            E.wait("ld_lnb_col")
            E.do("DVE", lambda e: e.tensor_copy(lnbc_bf[:, :], lnbc[:, :]), event="lnb_bf")

        run_phase({"sync": p0_sync, "vector": p0_vec}, name="P0")
        if PHASES < 1:
            return nc, ctx

        p1 = contextlib.ExitStack()
        wq_b = p1.enter_context(nc.sbuf_tensor("wq_b", [128, 8 * D], BF16))
        wk_b = p1.enter_context(nc.sbuf_tensor("wk_b", [128, 8 * D], BF16))
        wv_b = p1.enter_context(nc.sbuf_tensor("wv_b", [128, 8 * D], BF16))
        permM = p1.enter_context(nc.sbuf_tensor("perm_s", [128, 128], BF16))
        wst = [p1.enter_context(nc.sbuf_tensor(f"wst{i}", [128, D], F32)) for i in range(2)]
        gst = [p1.enter_context(nc.sbuf_tensor(f"gst{i}", [128, 2048], F32)) for i in range(2)]
        gcols = p1.enter_context(nc.sbuf_tensor("gcols", [128, 32], F32))
        tabC_s = [p1.enter_context(nc.sbuf_tensor(f"tabC{i}", [128, CH], F32)) for i in range(3)]
        tabS_s = [p1.enter_context(nc.sbuf_tensor(f"tabS{i}", [128, CH], F32)) for i in range(3)]
        xbuf = [p1.enter_context(nc.sbuf_tensor(f"xbuf{i}", [128, D], F32)) for i in range(6)]
        sq_scr = p1.enter_context(nc.sbuf_tensor("sq_scr", [128, D], F32))
        sc = p1.enter_context(nc.sbuf_tensor("sc", [128, 128], F32))
        xn_tok = [p1.enter_context(nc.sbuf_tensor(f"xn_tok{i}", [128, D], BF16)) for i in range(8)]
        xnT = [p1.enter_context(nc.sbuf_tensor(f"xnT{i}", [128, 8 * CH], BF16)) for i in range(2)]
        xqb = [p1.enter_context(nc.sbuf_tensor(f"xqb{i}", [128, CH], BF16)) for i in range(2)]
        tmpa = [p1.enter_context(nc.sbuf_tensor(f"tmpa{i}", [128, CH], F32)) for i in range(2)]
        tmpb = [p1.enter_context(nc.sbuf_tensor(f"tmpb{i}", [128, CH], F32)) for i in range(2)]
        qs = [p1.enter_context(nc.sbuf_tensor(f"qs{i}", [128, 8 * CH], BF16)) for i in range(2)]
        ks = [p1.enter_context(nc.sbuf_tensor(f"ks{i}", [128, 8 * CH], BF16)) for i in range(2)]
        vs = [p1.enter_context(nc.sbuf_tensor(f"vs{i}", [128, 4 * 1040], BF16)) for i in range(2)]
        pQ = [p1.enter_context(nc.psum_tensor(f"pQ{i}", [128, 512], F32)) for i in range(4)]
        pS = [p1.enter_context(nc.psum_tensor(f"pS{i}", [128, 512], F32)) for i in range(2)]
        pT = [p1.enter_context(nc.psum_tensor(f"pT{i}", [128, 1024], BF16)) for i in range(2)]

        WMATS = [("q", wq_b, "wqT", 0.125), ("k", wk_b, "wkT", 1.0), ("v", wv_b, "wvT", 1.0)]
        wnames = [(m, i) for m, _, _, _ in WMATS for i in range(8)]

        def p1_sync(E):
            if P1QKV:
                E.do("cPRM", lambda e: e.dma_start(out=permM[:, :], in_=io["permM"][:, :]), 16,
                     event="permld")
            for c in (range(3) if P1QKV else ()):
                E.do(f"tab{c % 3}", lambda e, c=c: e.dma_start(
                    out=tabC_s[c % 3][:, :], in_=io["tabC"][:, c * CH:(c + 1) * CH]), 16)
                E.do(f"tab{c % 3}", lambda e, c=c: e.dma_start(
                    out=tabS_s[c % 3][:, :], in_=io["tabS"][:, c * CH:(c + 1) * CH]),
                    16, event=f"tabS{c}")
            for g in range(4):
                E.do(f"xb{g % 5}", lambda e, g=g: e.dma_start(
                    out=xbuf[g % 6][:, :],
                    in_=io["x"].rearrange("(t p) m -> p t m", p=128)[:, g, :]),
                    16, event=f"xload{g}")
            for wi, (m, i) in enumerate(wnames if P1QKV else []):
                if wi >= 2:
                    pm, pi = wnames[wi - 2]
                    E.wait(f"wcast_{pm}{pi}")
                src = {"q": "wqT", "k": "wkT", "v": "wvT"}[m]
                E.do(f"wl{wi % 2}", lambda e, src=src, i=i, wi=wi: e.dma_start(
                    out=wst[wi % 2][:, :],
                    in_=io[src].rearrange("(a p) n -> p a n", p=128)[:, i, :]),
                    16, event=f"wload_{m}{i}")
                if wi == 7:
                    for g in (4, 5):
                        E.do(f"xb{g % 5}", lambda e, g=g: e.dma_start(
                            out=xbuf[g % 6][:, :],
                            in_=io["x"].rearrange("(t p) m -> p t m", p=128)[:, g, :]),
                            16, event=f"xload{g}")
            for c in range(0, NCH):
                for tt in range(4):
                    g = 4 * c + tt
                    if g < 6:
                        continue
                    E.wait(f"apply{(g - 6) // 4}_{(g - 6) % 4}")
                    E.do(f"xb{g % 5}", lambda e, g=g: e.dma_start(
                        out=xbuf[g % 6][:, :],
                        in_=io["x"].rearrange("(t p) m -> p t m", p=128)[:, g, :]),
                        16, event=f"xload{g}")
                c2 = c + 1
                if P1QKV and 3 <= c2 < NCH:
                    E.wait(f"rope{c2 - 3}k7")
                    E.do(f"tab{c2 % 3}", lambda e, c2=c2: e.dma_start(
                        out=tabC_s[c2 % 3][:, :], in_=io["tabC"][:, c2 * CH:(c2 + 1) * CH]), 16)
                    E.do(f"tab{c2 % 3}", lambda e, c2=c2: e.dma_start(
                        out=tabS_s[c2 % 3][:, :], in_=io["tabS"][:, c2 * CH:(c2 + 1) * CH]),
                        16, event=f"tabS{c2}")

        def p1_gp_rope_chunk(E, c):
            if not (P1QKV and P1ROPE):
                return
            for m in ("q", "k"):
                for i in range(8):
                    ri = c * 16 + (0 if m == "q" else 8) + i
                    if ri >= 2:
                        pc2, pm2, pi3 = ((c, m, i - 2) if i >= 2 else
                                         ((c, "q", i + 6) if m == "k" else (c - 1, "k", i + 6)))
                        E.wait(f"rope{pc2}{pm2}{pi3}")
                    E.wait(f"xq{c}{m}{i}", f"tabS{c}")
                    E.do("GP", lambda e, c=c, i=i, ri=ri: e.tensor_tensor(
                        tmpa[ri % 2][:, :], xqb[i % 2][:, :], tabC_s[c % 3][:, :], ALU.mult),
                        event=f"ra{c}{m}{i}")

        def p1_gp(E):
            for c in range(NCH):
                if c >= 1:
                    p1_gp_rope_chunk(E, c - 1)
                for j in (range(4 * c, 4 * c + 4) if P1GAMMA else ()):
                    if j >= 2:
                        E.wait(f"gred{j - 2}")
                    if j < 16:
                        E.do(f"gl{j % 2}", lambda e, j=j: e.dma_start(
                            out=gst[j % 2][:, :],
                            in_=io["wfcT"].rearrange("(a p) n -> p a n", p=128)
                            [:, j // 2, (j % 2) * 2048:(j % 2 + 1) * 2048]),
                            16, event=f"gload{j}")
                    else:
                        E.do(f"gl{j % 2}", lambda e, j=j: e.dma_start(
                            out=gst[j % 2][:, :].rearrange("p (a n) -> p a n", n=D),
                            in_=io["wprT"].rearrange("(a p) n -> p a n", p=128)
                            [:, 2 * (j - 16):2 * (j - 16) + 2, :]),
                            16, event=f"gload{j}")
            p1_gp_rope_chunk(E, NCH - 1)

        def p1_scalar(E):
            def emit_wcasts(lo, hi):
                for wi, (m, i) in list(enumerate(wnames if P1QKV else []))[lo:hi]:
                    dstb = {"q": wq_b, "k": wk_b, "v": wv_b}[m]
                    scale = 0.125 if m == "q" else 1.0
                    E.wait(f"wload_{m}{i}")
                    if scale == 1.0:
                        E.do("ACT", lambda e, dstb=dstb, i=i, wi=wi: e.activation(
                            dstb[:, i * D:(i + 1) * D], wst[wi % 2][:, :], AF.Copy),
                            event=f"wcast_{m}{i}")
                    else:
                        E.do("ACT", lambda e, dstb=dstb, i=i, wi=wi, scale=scale: e.activation(
                            dstb[:, i * D:(i + 1) * D], wst[wi % 2][:, :], AF.Copy, scale=scale),
                            event=f"wcast_{m}{i}")
            xq_rel = []

            def emit_smalls(c):
                b = (c % 2) * 32
                E.wait(f"veps{c}")
                E.do("ACT", lambda e, b=b: e.activation(
                    sc[:, b + 20:b + 24], sc[:, b + 20:b + 24], AF.Ln), event=f"lnx{c}")
                E.wait(f"lnx{c}")
                E.do("ACT", lambda e, b=b: e.activation(
                    sc[:, b + 24:b + 28], sc[:, b + 20:b + 24], AF.Exp, scale=-0.5),
                    event=f"rstd{c}")
                E.wait(f"nb{c}", f"rstd{c}")
                for tt in range(4):
                    g = c * 4 + tt
                    if g >= 8:
                        c2, tt2 = divmod(g - 8, 4)
                        E.wait(f"tr{c2}_{tt2}_7")
                    E.do("ACT", lambda e, c=c, tt=tt, b=b, g=g: e.activation(
                        xn_tok[g % 8][:, :], xbuf[g % 6][:, :], AF.Identity,
                        scale=sc[:, b + 24 + tt:b + 25 + tt], bias=sc[:, b + 28 + tt:b + 29 + tt]),
                        event=f"apply{c}_{tt}")

            def emit_copies(c):
                E.wait(f"trev{c}_3_7")
                E.do(f"xnsp{c % 2}", lambda e, c=c: e.dma_start(
                    out=io["xn_sp"].rearrange("(a p) t -> p a t", p=128)[:, :, c * CH:(c + 1) * CH],
                    in_=xnT[c % 2][:, :].rearrange("p (a t) -> p a t", t=CH)),
                    16, event=f"xnsp{c}")
                if not P1QKV:
                    return
                for mi, m in enumerate(("q", "k") if P1ROPE else ()):
                    for i in range(8):
                        u = c * 24 + mi * 8 + i
                        E.wait(f"pmm{c}{m}{i}")
                        if len(xq_rel) >= 2:
                            E.wait(*xq_rel[-2])
                        E.do("ACT", lambda e, i=i, u=u: e.activation(
                            xqb[i % 2][:, :], pQ[u % NQB][:, :], AF.Copy),
                            event=f"xq{c}{m}{i}")
                        xq_rel.append((f"perm{c}{m}{i}", f"ra{c}{m}{i}"))
                for uu in (range(8) if P1V else ()):
                    tt, vc = divmod(uu, 2)
                    u = c * 24 + 16 + uu
                    E.wait(f"vmm{c}_{tt}_{vc}")
                    E.do("ACT", lambda e, c=c, tt=tt, vc=vc, u=u: e.activation(
                        vs[c % 2][:, tt * 1040 + vc * 520:tt * 1040 + vc * 520 + 520]
                        .rearrange("p (h z) -> p h z", z=65)[:, :, 0:64],
                        pQ[u % NQB][:, :].rearrange("p (h z) -> p h z", z=64), AF.Copy),
                        event=f"vcp{c}_{tt}_{vc}")
                E.wait(f"rope{c}q7", f"rope{c}k7")
                E.do(f"qsp{c % 2}", lambda e, c=c: e.dma_start(
                    out=io["qT_sp"].rearrange("(a p) t -> p a t", p=128)[:, :, c * CH:(c + 1) * CH],
                    in_=qs[c % 2][:, :].rearrange("p (a t) -> p a t", t=CH)),
                    16, event=f"qsp{c}")
                E.do(f"ksp{c % 2}", lambda e, c=c: e.dma_start(
                    out=io["kT_sp"].rearrange("(a p) t -> p a t", p=128)[:, :, c * CH:(c + 1) * CH],
                    in_=ks[c % 2][:, :].rearrange("p (a t) -> p a t", t=CH)),
                    16, event=f"ksp{c}")
                if P1V:
                    E.wait(f"vcp{c}_3_1")
                    E.do(f"vsp{c % 2}", lambda e, c=c: e.dma_start(
                        out=io["v_sp"].rearrange("(ct p) n -> p ct n", p=128)[:, c * 4:(c + 1) * 4, :],
                        in_=vs[c % 2][:, :].rearrange("p (a n) -> p a n", n=1040)),
                        16, event=f"vsp{c}")

            emit_wcasts(0, 8)
            emit_smalls(0)
            emit_wcasts(8, 24)
            for c in range(1, NCH):
                emit_smalls(c)
                emit_copies(c - 1)
            emit_copies(NCH - 1)

        def p1_vec(E):
            for i in (range(2) if P1QKV else ()):
                E.do("DVE", lambda e, i=i: e.memset(vs[i][:, :], 1.0), event=f"vsones{i}")

            def emit_stats(c):
                b = (c % 2) * 32
                for tt in range(4):
                    g = c * 4 + tt
                    E.wait(f"xload{g}")
                    if g >= 1:
                        pg = g - 1
                        E.wait(f"stats{pg // 4}_{pg % 4}")
                    xt = xbuf[g % 6]
                    E.do("DVE", lambda e, xt=xt, g=g, b=b, tt=tt: e.tensor_scalar(
                        sq_scr[:, 0:D], xt[:, :], 1.0, 0.0, ALU.mult, ALU.add,
                        accum_out=sc[:, b + tt:b + tt + 1]))
                    E.do("DVE", lambda e, xt=xt, g=g, b=b, tt=tt: e.scalar_tensor_tensor(
                        sq_scr[:, 0:D], xt[:, :], 1.0, xt[:, :],
                        ALU.mult, ALU.mult, accum_out=sc[:, b + 4 + tt:b + 4 + tt + 1]),
                        event=f"stats{c}_{tt}")
                E.wait(f"stats{c}_3")
                E.do("DVE", lambda e, b=b: e.tensor_scalar(
                    sc[:, b + 8:b + 12], sc[:, b + 0:b + 4], 1.0 / D, None, ALU.mult),
                    event=f"mux{c}")
                E.do("DVE", lambda e, b=b: e.tensor_scalar(
                    sc[:, b + 12:b + 16], sc[:, b + 4:b + 8], 1.0 / D, None, ALU.mult),
                    event=f"m2x{c}")
                E.wait(f"mux{c}")
                E.do("DVE", lambda e, b=b: e.tensor_tensor(
                    sc[:, b + 16:b + 20], sc[:, b + 8:b + 12], sc[:, b + 8:b + 12], ALU.mult),
                    event=f"msq{c}")
                E.wait(f"m2x{c}", f"msq{c}")
                E.do("DVE", lambda e, b=b: e.scalar_tensor_tensor(
                    sc[:, b + 20:b + 24], sc[:, b + 12:b + 16], LN_EPS, sc[:, b + 16:b + 20],
                    ALU.add, ALU.subtract), event=f"veps{c}")

            def emit_nb(c):
                b = (c % 2) * 32
                E.wait(f"rstd{c}")
                E.do("DVE", lambda e, b=b: e.scalar_tensor_tensor(
                    sc[:, b + 28:b + 32], sc[:, b + 8:b + 12], -1.0, sc[:, b + 24:b + 28],
                    ALU.mult, ALU.mult), event=f"nb{c}")

            def emit_greds(c):
                for j in (range(4 * c, 4 * c + 4) if P1GAMMA else ()):
                    E.wait(f"gload{j}")
                    E.do("DVE", lambda e, j=j: e.tensor_reduce(
                        gcols[:, j:j + 1], gst[j % 2][:, :], AX.X, ALU.add,
                        apply_absolute_value=True), event=f"gred{j}")

            def emit_trevs(c):
                for tt in range(4):
                    for dt in range(8):
                        E.wait(f"tr{c}_{tt}_{dt}")
                        if c >= 2 and tt == 0 and dt == 0:
                            if P1QKV and P1V:
                                E.wait(f"vmm{c - 2}_3_1")
                            elif P1QKV:
                                E.wait(f"rope{c - 2}k7")
                            E.wait(f"xnsp{c - 2}")
                        E.do("DVE", lambda e, c=c, tt=tt, dt=dt: e.tensor_copy(
                            xnT[c % 2][:, dt * CH + tt * 128:dt * CH + (tt + 1) * 128],
                            pT[dt % 2][:, 0:128]), event=f"trev{c}_{tt}_{dt}")

            def emit_ropes(c):
                if not P1QKV:
                    return
                E.wait(f"tabS{c}")
                for m in ("q", "k"):
                    dst = qs if m == "q" else ks
                    for i in range(8):
                        ri = c * 16 + (0 if m == "q" else 8) + i
                        if not P1ROPE:
                            u = c * 24 + (0 if m == "q" else 8) + i
                            E.wait(f"pmm{c}{m}{i}")
                            if c >= 2:
                                E.wait(f"qsp{c - 2}" if m == "q" else f"ksp{c - 2}")
                            E.do("DVE", lambda e, dst=dst, c=c, i=i, u=u: e.tensor_copy(
                                dst[c % 2][:, i * CH:(i + 1) * CH], pQ[u % NQB][:, :]),
                                event=f"rope{c}{m}{i}")
                            if E.dry:
                                ctx.events.setdefault(f"ra{c}{m}{i}", ctx.events[f"rope{c}{m}{i}"])
                                ctx.events.setdefault(f"perm{c}{m}{i}", ctx.events[f"rope{c}{m}{i}"])
                                ctx.events.setdefault(f"rb{c}{m}{i}", ctx.events[f"rope{c}{m}{i}"])
                                ctx.events.setdefault(f"xq{c}{m}{i}", ctx.events[f"rope{c}{m}{i}"])
                            continue
                        E.wait(f"pmm{c}{m}{i}")
                        if ri >= 2:
                            pc2, pm2, pi3 = ((c, m, i - 2) if i >= 2 else
                                             ((c, "q", i + 6) if m == "k" else (c - 1, "k", i + 6)))
                            E.wait(f"rope{pc2}{pm2}{pi3}")
                        E.wait(f"perm{c}{m}{i}")
                        E.do("DVE", lambda e, c=c, i=i, ri=ri: e.tensor_tensor(
                            tmpb[ri % 2][:, :], pS[i % 2][:, :], tabS_s[c % 3][:, :], ALU.mult),
                            event=f"rb{c}{m}{i}")
                        E.wait(f"ra{c}{m}{i}", f"rb{c}{m}{i}")
                        if c >= 2:
                            E.wait(f"qsp{c - 2}" if m == "q" else f"ksp{c - 2}")
                        E.do("DVE", lambda e, dst=dst, c=c, i=i, ri=ri: e.tensor_add(
                            dst[c % 2][:, i * CH:(i + 1) * CH], tmpa[ri % 2][:, :], tmpb[ri % 2][:, :]),
                            event=f"rope{c}{m}{i}")

            for c in range(NCH):
                emit_stats(c)
                if c >= 1:
                    emit_trevs(c - 1)
                emit_nb(c)
                if c >= 1:
                    emit_ropes(c - 1)
                emit_greds(c)
            emit_trevs(NCH - 1)
            emit_ropes(NCH - 1)
            if not P1GAMMA:
                return
            E.wait("gfold_fc")
            E.do("DVE", lambda e: e.tensor_reduce(thr[0:1, 4:5], pS[0][0:1, 0:16], AX.X, ALU.add),
                 event="gtot_fc")
            E.wait("gtot_fc")
            E.do("DVE", lambda e: e.tensor_scalar(
                thr[0:1, 0:1], thr[0:1, 4:5], 0.5 / (D * HID), 0.5 * ABS_EPS,
                ALU.mult, ALU.add), event="thr_fc")
            E.wait("gfold_pr")
            E.do("DVE", lambda e: e.tensor_reduce(thr[0:1, 5:6], pS[0][0:1, 16:32], AX.X, ALU.add),
                 event="gtot_pr")
            E.wait("gtot_pr")
            E.do("DVE", lambda e: e.tensor_scalar(
                thr[0:1, 2:3], thr[0:1, 5:6], 0.5 / (D * HID), 0.5 * ABS_EPS,
                ALU.mult, ALU.add), event="thr_pr")
            E.wait("bc_fc")
            E.do("DVE", lambda e: e.tensor_copy(thrB[:, 0:1], pS[0][:, 40:41]))
            E.do("DVE", lambda e: e.tensor_scalar(
                thrB[:, 1:2], pS[0][:, 40:41], -1.0, None, ALU.mult), event="thrB_fc")
            E.wait("bc_pr")
            E.do("DVE", lambda e: e.tensor_copy(thrB[:, 2:3], pS[0][:, 41:42]))
            E.do("DVE", lambda e: e.tensor_scalar(
                thrB[:, 3:4], pS[0][:, 41:42], -1.0, None, ALU.mult), event="thrB_pr")

        def p1_tensor(E):
            grp = []

            def emit_tr(c):
                for tt in range(4):
                    E.wait(f"apply{c}_{tt}")
                    for dt in range(8):
                        gi = c * 32 + tt * 8 + dt
                        if gi >= 2:
                            pc, r = divmod(gi - 2, 32)
                            ptt, pdt = divmod(r, 8)
                            E.wait(f"trev{pc}_{ptt}_{pdt}")
                        E.do("PE", lambda e, c=c, tt=tt, dt=dt: e.transpose(
                            pT[dt % 2][:, 0:128], xn_tok[(c * 4 + tt) % 8][:, dt * 128:(dt + 1) * 128],
                            idb[:, :]), event=f"tr{c}_{tt}_{dt}")

            def emit_qkv(c):
                for m, wb, _, _ in (WMATS if P1QKV else []):
                    if m in ("q", "k"):
                        for i in range(8):
                            u = len(grp)
                            if u >= NQB:
                                E.wait(grp[u - NQB])
                            E.wait(f"wcast_{m}7", f"trev{c}_3_7")
                            for dt in range(8):
                                E.do("PE", lambda e, wb=wb, i=i, dt=dt, c=c, u=u: e.matmul(
                                    pQ[u % NQB][:, :], wb[:, dt * D + i * 128:dt * D + (i + 1) * 128],
                                    xnT[c % 2][:, dt * CH:(dt + 1) * CH],
                                    start=(dt == 0), stop=(dt == 7)),
                                    inc=1 if dt == 7 else 0,
                                    event=f"pmm{c}{m}{i}" if dt == 7 else None)
                            grp.append(f"xq{c}{m}{i}" if P1ROPE else f"ra{c}{m}{i}")
                            if not P1ROPE:
                                continue
                            if not P1PERM:
                                if E.dry:
                                    ctx.events[f"perm{c}{m}{i}"] = ctx.events[f"pmm{c}{m}{i}"]
                                continue
                            E.wait(f"xq{c}{m}{i}")
                            if not (c == 0 and m == "q" and i < 2):
                                pi = i - 2
                                pm, pc2 = m, c
                                if pi < 0:
                                    pm = "k" if m == "q" else "q"
                                    pc2 = c - 1 if m == "q" else c
                                    pi += 8
                                E.wait(f"rb{pc2}{pm}{pi}")
                            E.wait("permld")
                            E.do("PE", lambda e, i=i: e.matmul(
                                pS[i % 2][:, :], permM[:, :], xqb[i % 2][:, :],
                                start=True, stop=True), event=f"perm{c}{m}{i}")
                    else:
                        if not P1V:
                            continue
                        for uu in range(8):
                            tt, vc = divmod(uu, 2)
                            u = len(grp)
                            if u >= NQB:
                                E.wait(grp[u - NQB])
                            E.wait("wcast_v7")
                            if c < 2:
                                E.wait(f"vsones{c % 2}")
                            else:
                                E.wait(f"vsp{c - 2}")
                            for dt in range(8):
                                E.do("PE", lambda e, wb=wb, tt=tt, vc=vc, dt=dt, c=c, u=u: e.matmul(
                                    pQ[u % NQB][:, :],
                                    xnT[c % 2][:, dt * CH + tt * 128:dt * CH + (tt + 1) * 128],
                                    wb[:, dt * D + vc * 512:dt * D + (vc + 1) * 512],
                                    start=(dt == 0), stop=(dt == 7)),
                                    inc=1 if dt == 7 else 0,
                                    event=f"vmm{c}_{tt}_{vc}" if dt == 7 else None)
                            grp.append(f"vcp{c}_{tt}_{vc}")

            for c in range(NCH):
                if c >= 1:
                    emit_qkv(c - 1)
                emit_tr(c)
            emit_qkv(NCH - 1)
            if not P1GAMMA:
                return
            for j in range(16):
                E.wait(f"gred{j}")
            if P1QKV:
                E.wait(f"rb{NCH - 1}k6")
            E.do("PE", lambda e: e.matmul(pS[0][0:1, 0:16], ones_f[:, 0:1], gcols[:, 0:16],
                                          start=True, stop=True), event="gfold_fc")
            E.wait("gtot_fc")
            for j in range(16, 32):
                E.wait(f"gred{j}")
            E.do("PE", lambda e: e.matmul(pS[0][0:1, 16:32], ones_f[:, 0:1], gcols[:, 16:32],
                                          start=True, stop=True), event="gfold_pr")
            E.wait("gtot_pr", "thr_fc")
            E.do("PE", lambda e: e.matmul(pS[0][:, 40:41], ones_f[0:1, :], thr[0:1, 0:1],
                                          start=True, stop=True), event="bc_fc")
            E.wait("thr_pr", "thrB_fc")
            E.do("PE", lambda e: e.matmul(pS[0][:, 41:42], ones_f[0:1, :], thr[0:1, 2:3],
                                          start=True, stop=True), event="bc_pr")

        run_phase({"sync": p1_sync, "gpsimd": p1_gp, "scalar": p1_scalar,
                   "vector": p1_vec, "tensor": p1_tensor}, name="P1")
        p1.close()
        if PHASES < 2:
            return nc, ctx

        p2w = contextlib.ExitStack()
        wfc_q = p2w.enter_context(nc.sbuf_tensor("wfc_q", [128, 8 * HID], BF16))
        wpr_q = p2w.enter_context(nc.sbuf_tensor("wpr_q", [128, 32 * D], BF16))
        ubf = p2w.enter_context(nc.sbuf_tensor("ubf", [1, D], BF16))
        vbs = p2w.enter_context(nc.sbuf_tensor("vbs", [128, D], F32))

        p2 = contextlib.ExitStack()
        xnf = p2.enter_context(nc.sbuf_tensor("xnf", [128, 8 * CH], BF16))
        hbuf = p2.enter_context(nc.sbuf_tensor("hbuf", [128, 32 * CH], BF16))
        hsq = [p2.enter_context(nc.sbuf_tensor(f"hsq{i}", [128, CH], BF16)) for i in range(2)]
        sst = p2.enter_context(nc.sbuf_tensor("sst", [128, 4 * CH], F32))
        sstb = p2.enter_context(nc.sbuf_tensor("sstb", [1, CH], BF16))
        ist = p2.enter_context(nc.sbuf_tensor("ist", [128, 8], F32))
        gbuf = p2.enter_context(nc.sbuf_tensor("gbuf", [128, 4 * D], BF16))
        pH = [p2.enter_context(nc.psum_tensor(f"pH{i}", [128, 512], F32)) for i in range(2)]
        pStats = p2.enter_context(nc.psum_tensor("pStats", [128, 1024], F32))
        pP = [p2.enter_context(nc.psum_tensor(f"pP{i}", [128, 512], F32)) for i in range(2)]
        pIst = p2.enter_context(nc.psum_tensor("pIst", [128, 512], F32))

        p2a = contextlib.ExitStack()
        gst2 = [p2a.enter_context(nc.sbuf_tensor(f"gst2_{i}", [128, 1024], F32)) for i in range(2)]
        qtmp = p2a.enter_context(nc.sbuf_tensor("qtmp", [128, 1024], F32))
        vf = vbs
        pV2 = pH
        pU2 = pStats

        def p2a_sync(E):
            for j in range(64):
                if j >= 2:
                    E.wait(f"qz{j - 2}")
                if j < 32:
                    E.do(f"g2l{j % 2}", lambda e, j=j: e.dma_start(
                        out=gst2[j % 2][:, :],
                        in_=io["wfcT"].rearrange("(a p) (u n) -> p a u n", p=128, n=1024)
                        [:, j // 4, j % 4, :]),
                        16, event=f"g2load{j}")
                else:
                    E.do(f"g2l{j % 2}", lambda e, j=j: e.dma_start(
                        out=gst2[j % 2][:, :],
                        in_=io["wprT"].rearrange("(a p) n -> p a n", p=128)[:, j - 32, :]),
                        16, event=f"g2load{j}")

        def p2a_vec(E):
            for j in range(64):
                E.wait(f"g2load{j}", "thrB_fc" if j < 32 else "thrB_pr", "lnb_bf")
                if j >= 1:
                    E.wait(f"qz{j - 1}")
                toff = 0 if j < 32 else 2
                E.do("DVE", lambda e, j=j, toff=toff: e.tensor_scalar(
                    qtmp[:, :], gst2[j % 2][:, :], thrB[:, toff + 1:toff + 2], None, ALU.is_le),
                    event=f"qt{j}")
                E.wait(f"qt{j}")
                if j < 32:
                    E.do("DVE", lambda e, j=j, toff=toff: e.scalar_tensor_tensor(
                        wfc_q[:, (j // 4) * HID + (j % 4) * 1024:(j // 4) * HID + (j % 4 + 1) * 1024],
                        gst2[j % 2][:, :],
                        thrB[:, toff:toff + 1], qtmp[:, :], ALU.is_ge, ALU.subtract),
                        event=f"qz{j}")
                else:
                    E.do("DVE", lambda e, j=j, toff=toff: e.scalar_tensor_tensor(
                        wpr_q[:, (j - 32) * D:(j - 31) * D], gst2[j % 2][:, :],
                        thrB[:, toff:toff + 1], qtmp[:, :], ALU.is_ge, ALU.subtract),
                        event=f"qz{j}")
            for jt in range(32):
                E.wait(f"vmm2_{jt}")
                E.do("DVE", lambda e, jt=jt: e.tensor_scalar(
                    wpr_q[:, jt * D:(jt + 1) * D], wpr_q[:, jt * D:(jt + 1) * D],
                    lnwc[:, jt:jt + 1], None, ALU.mult), event=f"wsc{jt}")
            E.wait("vbs_mm")
            E.do("DVE", lambda e: e.tensor_copy(vbs[:, 0:512], pV2[0][:, 0:512]))
            E.do("DVE", lambda e: e.tensor_copy(vbs[:, 512:D], pV2[1][:, 0:512]), event="vbs")

        def p2a_scalar(E):
            E.wait("uv_v")
            E.do("ACT", lambda e: e.activation(vf[0:1, 0:512], pV2[0][0:1, 0:512], AF.Copy))
            E.do("ACT", lambda e: e.activation(vf[0:1, 512:D], pV2[1][0:1, 0:512], AF.Copy), event="vf")
            E.wait("uv_u")
            E.do("ACT", lambda e: e.activation(ubf[0:1, :], pU2[0:1, 0:1024], AF.Copy), event="ubf")

        def p2a_tensor(E):
            for jt in range(32):
                E.wait(f"qz{32 + jt}")
                for half in range(2):
                    E.do("PE", lambda e, jt=jt, half=half: e.matmul(
                        pV2[half][0:1, 0:512], lnbc_bf[:, jt:jt + 1],
                        wpr_q[:, jt * D + half * 512:jt * D + (half + 1) * 512],
                        start=(jt == 0), stop=(jt == 31)),
                        inc=1 if half == 1 else 0,
                        event=f"vmm2_{jt}" if half == 1 else None)
            for jt in range(32):
                E.wait(f"wsc{jt}")
                for half in range(2):
                    E.do("PE", lambda e, jt=jt, half=half: e.matmul(
                        pU2[0:1, half * 512:(half + 1) * 512], ones_b[:, 0:1],
                        wpr_q[:, jt * D + half * 512:jt * D + (half + 1) * 512],
                        start=(jt == 0), stop=(jt == 31)),
                        inc=1 if (jt == 31 and half == 1) else 0,
                        event="uv_u" if (jt == 31 and half == 1) else None)
            E.wait("vf")
            E.do("PE", lambda e: e.matmul(
                pV2[0][:, 0:512], ones_f[0:1, :], vf[0:1, 0:512], start=True, stop=True), inc=0)
            E.do("PE", lambda e: e.matmul(
                pV2[1][:, 0:512], ones_f[0:1, :], vf[0:1, 512:D], start=True, stop=True),
                event="vbs_mm")

        def p2a_post_dry():
            ctx.events["uv_v"] = ctx.events["vmm2_31"]

        S_MU, S_M2, S_VEPS, S_RSTD = 0, CH, 2 * CH, 3 * CH
        S_LN = S_VEPS

        def p2_sync(E):
            for c in range(NCH):
                if c >= 1:
                    E.wait(f"hmm{c - 1}_31")
                E.do("xnl", lambda e, c=c: e.dma_start(
                    out=xnf[:, :].rearrange("p (a t) -> p a t", t=CH),
                    in_=io["xn_sp"].rearrange("(a p) t -> p a t", p=128)[:, :, c * CH:(c + 1) * CH]),
                    16, event=f"xnload{c}")

        def p2_vec(E):
            E.do("DVE", lambda e: e.memset(sst[:, :], 0.0), event="sstz")
            for c in range(NCH):
                for jt in range(32):
                    E.wait(f"gelu{c}_{jt}")
                    if c * 32 + jt >= 2:
                        pj = c * 32 + jt - 2
                        E.wait(f"ssq{pj // 32}_{pj % 32}")
                    E.do("DVE", lambda e, c=c, jt=jt: e.tensor_tensor(
                        hsq[jt % 2][:, :], hbuf[:, jt * CH:(jt + 1) * CH],
                        hbuf[:, jt * CH:(jt + 1) * CH], ALU.mult), event=f"hsq{c}_{jt}")
                E.wait(f"smm{c}")
                if c >= 1:
                    E.wait(f"nmu{c - 1}")
                E.do("DVE", lambda e: e.tensor_scalar(
                    sst[0:1, S_MU:S_MU + CH], pStats[0:1, 0:512], 1.0 / HID, None, ALU.mult),
                    event=f"fmu_{c}")
                E.do("DVE", lambda e: e.tensor_scalar(
                    sst[0:1, S_M2:S_M2 + CH], pStats[0:1, 512:1024], 1.0 / HID, None, ALU.mult),
                    event=f"m2_{c}")
                E.wait(f"fmu_{c}")
                E.do("DVE", lambda e: e.tensor_tensor(
                    sst[0:1, S_VEPS:S_VEPS + CH], sst[0:1, S_MU:S_MU + CH],
                    sst[0:1, S_MU:S_MU + CH], ALU.mult), event=f"fmsq_{c}")
                E.wait(f"m2_{c}", f"fmsq_{c}")
                E.do("DVE", lambda e: e.scalar_tensor_tensor(
                    sst[0:1, S_VEPS:S_VEPS + CH], sst[0:1, S_M2:S_M2 + CH], LN_EPS,
                    sst[0:1, S_VEPS:S_VEPS + CH], ALU.add, ALU.subtract), event=f"fveps_{c}")
                E.wait(f"frstd_{c}")
                if c >= 1:
                    E.wait(f"pmm2_{c - 1}_3_1")
                E.do("DVE", lambda e: e.tensor_scalar(
                    sstb[0:1, :], sst[0:1, S_MU:S_MU + CH], -1.0, None, ALU.mult),
                    event=f"nmu{c}")
                for tt in range(4):
                    E.wait(f"istT{c}_{tt}")
                    E.do("DVE", lambda e, c=c, tt=tt: e.tensor_copy(
                        ist[:, (c % 2) * 4 + tt:(c % 2) * 4 + tt + 1], pIst[:, 0:1]),
                        event=f"ist{c}_{tt}")
                for tsub in range(4):
                    for ddc in range(2):
                        E.wait(f"pmm2_{c}_{tsub}_{ddc}", f"ist{c}_{tsub}")
                        if c >= 1:
                            E.wait(f"gsp{c - 1}")
                        E.do("DVE", lambda e, c=c, tsub=tsub, ddc=ddc: e.scalar_tensor_tensor(
                            gbuf[:, tsub * D + ddc * 512:tsub * D + (ddc + 1) * 512],
                            pP[(tsub * 2 + ddc) % 2][:, 0:512],
                            ist[:, (c % 2) * 4 + tsub:(c % 2) * 4 + tsub + 1],
                            vbs[:, ddc * 512:(ddc + 1) * 512], ALU.mult, ALU.add),
                            event=f"gevac{c}_{tsub}_{ddc}")

        def p2_scalar(E):
            for c in range(NCH):
                for jt in range(32):
                    E.wait(f"hmm{c}_{jt}")
                    if c >= 1 and jt == 0:
                        E.wait(f"pmm2_{c - 1}_3_1")
                    E.do("ACT", lambda e, c=c, jt=jt: e.activation(
                        hbuf[:, jt * CH:(jt + 1) * CH], pH[jt % 2][:, :], AF.Gelu),
                        event=f"gelu{c}_{jt}")
                E.wait(f"fveps_{c}")
                E.do("ACT", lambda e: e.activation(
                    sst[0:1, S_LN:S_LN + CH], sst[0:1, S_VEPS:S_VEPS + CH], AF.Ln),
                    event=f"fln_{c}")
                E.wait(f"fln_{c}")
                E.do("ACT", lambda e: e.activation(
                    sst[0:1, S_RSTD:S_RSTD + CH], sst[0:1, S_LN:S_LN + CH], AF.Exp, scale=-0.5),
                    event=f"frstd_{c}")
                for tsub in range(4):
                    for ddc in range(2):
                        E.wait(f"gevac{c}_{tsub}_{ddc}")
                E.do("gspS", lambda e, c=c: e.dma_start(
                    out=io["g_sp"].rearrange("(ct p) n -> p ct n", p=128)[:, c * 4:(c + 1) * 4, :],
                    in_=gbuf[:, :].rearrange("p (a n) -> p a n", n=D)),
                    16, event=f"gsp{c}")

        def p2_tensor(E):
            for c in range(NCH):
                E.wait(f"xnload{c}")
                for jt in range(32):
                    gi = c * 32 + jt
                    if gi == 0:
                        E.wait("qz31", "vbs")
                    if gi >= 2:
                        pj = gi - 2
                        E.wait(f"gelu{pj // 32}_{pj % 32}")
                    for dt in range(8):
                        E.do("PE", lambda e, c=c, jt=jt, dt=dt: e.matmul(
                            pH[jt % 2][:, :], wfc_q[:, dt * HID + jt * 128:dt * HID + (jt + 1) * 128],
                            xnf[:, dt * CH:(dt + 1) * CH],
                            start=(dt == 0), stop=(dt == 7)),
                            inc=1 if dt == 7 else 0, event=f"hmm{c}_{jt}" if dt == 7 else None)
                for jt in range(32):
                    E.wait(f"gelu{c}_{jt}")
                    if c >= 1 and jt == 0:
                        E.wait(f"m2_{c - 1}")
                    E.do("PE", lambda e, c=c, jt=jt: e.matmul(
                        pStats[0:1, 0:512], ones_b[:, 0:1], hbuf[:, jt * CH:(jt + 1) * CH],
                        start=(jt == 0), stop=(jt == 31)), inc=0)
                    E.wait(f"hsq{c}_{jt}")
                    E.do("PE", lambda e, c=c, jt=jt: e.matmul(
                        pStats[0:1, 512:1024], ones_b[:, 0:1], hsq[jt % 2][:, :],
                        start=(jt == 0), stop=(jt == 31)),
                        event=f"ssq{c}_{jt}")
                for tt in range(4):
                    E.wait(f"frstd_{c}")
                    gi = c * 4 + tt
                    if gi >= 1:
                        pv = gi - 1
                        E.wait(f"ist{pv // 4}_{pv % 4}")
                    E.do("PE", lambda e, tt=tt: e.transpose(
                        pIst[:, 0:128], sst[0:128, S_RSTD + tt * 128:S_RSTD + (tt + 1) * 128],
                        idf[:, :]), event=f"istT{c}_{tt}")
                for tsub in range(4):
                    for ddc in range(2):
                        u = tsub * 2 + ddc
                        gi = c * 8 + u
                        if gi >= 2:
                            pu = gi - 2
                            E.wait(f"gevac{pu // 8}_{(pu % 8) // 2}_{pu % 2}")
                        E.wait(f"nmu{c}")
                        for jt in range(32):
                            E.do("PE", lambda e, c=c, tsub=tsub, ddc=ddc, jt=jt: e.matmul(
                                pP[(tsub * 2 + ddc) % 2][:, 0:512],
                                hbuf[:, jt * CH + tsub * 128:jt * CH + (tsub + 1) * 128],
                                wpr_q[:, jt * D + ddc * 512:jt * D + (ddc + 1) * 512],
                                start=(jt == 0), stop=False), inc=0)
                        E.do("PE", lambda e, c=c, tsub=tsub, ddc=ddc: e.matmul(
                            pP[(tsub * 2 + ddc) % 2][:, 0:512],
                            sstb[0:1, tsub * 128:(tsub + 1) * 128],
                            ubf[0:1, ddc * 512:(ddc + 1) * 512],
                            start=False, stop=True),
                            event=f"pmm2_{c}_{tsub}_{ddc}")

        def p2_post_dry():
            p2a_post_dry()
            for c in range(NCH):
                ctx.events[f"smm{c}"] = ctx.events[f"ssq{c}_31"]

        def m2_sync(E):
            p2a_sync(E)
            p2_sync(E)

        def m2_vec(E):
            p2a_vec(E)
            p2_vec(E)

        def m2_scalar(E):
            p2a_scalar(E)
            p2_scalar(E)

        def m2_tensor(E):
            p2a_tensor(E)
            p2_tensor(E)

        run_phase({"sync": m2_sync, "scalar": m2_scalar, "vector": m2_vec,
                   "tensor": m2_tensor}, post_dry=p2_post_dry, name="P2")
        p2a.close()
        p2.close()
        p2w.close()
        if PHASES < 4:
            return nc, ctx

        p34 = contextlib.ExitStack()
        OT = p34.enter_context(nc.sbuf_tensor("OT", [128, 8 * T], BF16))
        p3 = contextlib.ExitStack()
        triM = p3.enter_context(nc.sbuf_tensor("tri_s", [128, 128], BF16))
        qp = [p3.enter_context(nc.sbuf_tensor(f"qp{i}", [128, T], BF16)) for i in range(2)]
        kp = [p3.enter_context(nc.sbuf_tensor(f"kp{i}", [128, T], BF16)) for i in range(2)]
        vp = [p3.enter_context(nc.sbuf_tensor(f"vp{i}", [128, 32 * 130], BF16)) for i in range(2)]
        pt = [p3.enter_context(nc.sbuf_tensor(f"pt{i}", [128, 1024], BF16)) for i in range(4)]
        rcp = p3.enter_context(nc.sbuf_tensor("rcp", [128, 1024], BF16))
        rbs = p3.enter_context(nc.sbuf_tensor("rbs", [128, 1024], F32))
        obt = [p3.enter_context(nc.sbuf_tensor(f"obt{i}", [128, T], BF16)) for i in range(2)]
        pS4 = p3.enter_context(nc.psum_tensor("pS4", [128, 2048], F32))
        pOAB = [p3.enter_context(nc.psum_tensor(f"pOAB{i}", [128, 1024], F32)) for i in range(2)]

        def p3_sync(E):
            E.do("cPRM", lambda e: e.dma_start(out=triM[:, :], in_=io["tri"][:, :]), 16,
                 event="trild")
            for p in range(8):
                if p >= 2:
                    E.wait(f"pairdone{p - 2}")
                E.do(f"pr{p % 2}", lambda e, p=p: e.dma_start(
                    out=qp[p % 2][:, :], in_=io["qT_sp"][p * 128:(p + 1) * 128, :]), 16)
                E.do(f"pr{p % 2}", lambda e, p=p: e.dma_start(
                    out=kp[p % 2][:, :], in_=io["kT_sp"][p * 128:(p + 1) * 128, :]), 16)
                E.do(f"pr{p % 2}", lambda e, p=p: e.dma_start(
                    out=vp[p % 2][:, :].rearrange("p (kt z) -> p kt z", z=130),
                    in_=io["v_sp"].rearrange("(kt p) n -> p kt n", p=128)[:, :, p * 130:(p + 1) * 130]),
                    16, event=f"vpl{p}")

        def p3_tensor(E):
            pending_rb = [None]
            last_rb_pair = {0: None, 1: None}

            def emit_rb(E, cur_p, cur_g0):
                if pending_rb[0] is None:
                    return
                p, qc, rbp = pending_rb[0]
                pending_rb[0] = None
                E.wait(f"recip{p}_{qc}")
                if cur_g0 is not None:
                    E.wait(f"exp{cur_p}_{cur_g0 + 1}")
                if not (p == 0 and qc == 0):
                    E.wait(f"rbscp{p if qc > 0 else p - 1}_{qc - 1 if qc > 0 else 7}")
                E.do("PE", lambda e, rbp=rbp: e.matmul(
                    pS4[0:64, rbp * 1024:rbp * 1024 + 512], ones_b[64:65, 0:64],
                    rcp[64:65, 0:512],
                    start=True, stop=True, tile_position=(64, 0)), inc=0)
                E.do("PE", lambda e, rbp=rbp: e.matmul(
                    pS4[0:64, rbp * 1024 + 512:rbp * 1024 + 1024], ones_b[64:65, 0:64],
                    rcp[64:65, 512:1024],
                    start=True, stop=True, tile_position=(64, 0)),
                    event=f"rbmm{p}_{qc}")
                last_rb_pair[rbp] = f"rbscp{p}_{qc}"

            for p in range(8):
                E.wait(f"vpl{p}")
                gkt = 0
                for qc in range(NCH):
                    g0 = gkt
                    n = nkt(qc)

                    def do_s(kt, p=p, qc=qc, g0=g0):
                        gk = g0 + kt
                        qtr = gk % 2
                        if gk >= 2:
                            E.wait(f"exp{p}_{gk - 2}")
                        if last_rb_pair[qtr] is not None:
                            E.wait(last_rb_pair[qtr])
                            last_rb_pair[qtr] = None
                        E.do("PE", lambda e, kt=kt, qtr=qtr: e.matmul(
                            pS4[:, qtr * 1024:qtr * 1024 + 512],
                            kp[p % 2][0:64, kt * 128:(kt + 1) * 128],
                            qp[p % 2][0:64, qc * CH:(qc + 1) * CH],
                            start=True, stop=True, tile_position=(0, 0)), inc=0)
                        E.do("PE", lambda e, kt=kt, qtr=qtr: e.matmul(
                            pS4[:, qtr * 1024 + 512:qtr * 1024 + 1024],
                            kp[p % 2][64:128, kt * 128:(kt + 1) * 128],
                            qp[p % 2][64:128, qc * CH:(qc + 1) * CH],
                            start=True, stop=True, tile_position=(64, 0)),
                            event=f"smm{p}_{gk}")

                    def do_pv(kt, p=p, qc=qc, g0=g0, n=n):
                        gk = g0 + kt
                        delta = max(0, kt * 128 - qc * CH)
                        E.wait(f"expm{p}_{gk}")
                        if kt == 0:
                            gq = p * NCH + qc
                            if gq >= 2:
                                pqc = gq - 2
                                E.wait(f"normA{pqc // NCH}_{pqc % NCH}",
                                       f"normB{pqc // NCH}_{pqc % NCH}")
                        E.do("PE", lambda e, kt=kt, delta=delta, qc=qc: e.matmul(
                            pOAB[qc % 2][0:65, delta:512], vp[p % 2][:, kt * 130:kt * 130 + 65],
                            pt[(g0 + kt) % 4][:, delta:512],
                            start=(kt == 0), stop=(kt == n - 1)), inc=0)
                        E.do("PE", lambda e, kt=kt, delta=delta, qc=qc: e.matmul(
                            pOAB[qc % 2][0:65, 512 + delta:1024],
                            vp[p % 2][:, kt * 130 + 65:kt * 130 + 130],
                            pt[(g0 + kt) % 4][:, 512 + delta:1024],
                            start=(kt == 0), stop=(kt == n - 1)),
                            event=f"pv{p}_{qc}_{kt}")

                    for kt in range(n):
                        do_s(kt)
                        if kt == 2:
                            emit_rb(E, p, g0)
                        if kt >= 2:
                            do_pv(kt - 2)
                    do_pv(n - 2)
                    do_pv(n - 1)
                    gkt += n
                    pending_rb[0] = (p, qc, (gkt + 1) % 2)
            emit_rb(E, None, None)

        def p3_scalar(E):
            for p in range(8):
                gkt = 0
                for qc in range(NCH):
                    for kt in range(nkt(qc)):
                        E.wait(f"smm{p}_{gkt}")
                        if gkt >= 4:
                            E.wait(f"ptfree{p}_{gkt - 4}")
                        E.do("ACT", lambda e, p=p, gkt=gkt: e.activation(
                            pt[gkt % 4][:, :], pS4[:, (gkt % 2) * 1024:(gkt % 2) * 1024 + 1024],
                            AF.Exp, bias=konst[:, 1:2]),
                            event=f"exp{p}_{gkt}")
                        gkt += 1

        def p3_vec(E):
            for p in range(8):
                gkt = 0
                for qc in range(NCH):
                    gkt += nkt(qc)
                    rbp = (gkt + 1) % 2
                    E.wait(f"pv{p}_{qc}_{nkt(qc) - 1}")
                    if not (p == 0 and qc == 0):
                        E.wait(f"rbmm{p if qc > 0 else p - 1}_{qc - 1 if qc > 0 else 7}")
                    E.do("DVE", lambda e, qc=qc: e.reciprocal(
                        rcp[64:65, 0:1024], pOAB[qc % 2][64:65, 0:1024]),
                         event=f"recip{p}_{qc}")
                    E.wait(f"rbmm{p}_{qc}")
                    E.do("DVE", lambda e, rbp=rbp: e.tensor_copy(
                        rbs[0:64, 0:1024], pS4[0:64, rbp * 1024:rbp * 1024 + 1024]),
                        event=f"rbscp{p}_{qc}")
                    E.wait(f"rbscp{p}_{qc}")
                    E.do("DVE", lambda e, p=p, qc=qc: e.tensor_mul(
                        OT[0:64, p * T + qc * CH:p * T + (qc + 1) * CH],
                        pOAB[qc % 2][0:64, 0:512], rbs[0:64, 0:512]), event=f"normA{p}_{qc}")
                    if p >= 2 and qc == 0:
                        E.wait(f"otbm{p - 2}")
                    E.do("DVE", lambda e, p=p, qc=qc: e.tensor_mul(
                        obt[p % 2][0:64, qc * CH:(qc + 1) * CH],
                        pOAB[qc % 2][0:64, 512:1024], rbs[0:64, 512:1024]), event=f"normB{p}_{qc}")

        def p3_gp(E):
            for p in range(8):
                gkt = 0
                for qc in range(NCH):
                    for kt in range(nkt(qc)):
                        gk = gkt
                        delta = max(0, kt * 128 - qc * CH)
                        if kt >= 4 * qc:
                            E.wait(f"exp{p}_{gk}", "trild")
                            E.do("GP", lambda e, gk=gk, delta=delta: e.tensor_mul(
                                pt[gk % 4][:, delta:delta + 128], pt[gk % 4][:, delta:delta + 128],
                                triM[:, :]), inc=0)
                            E.do("GP", lambda e, gk=gk, delta=delta: e.tensor_mul(
                                pt[gk % 4][:, 512 + delta:512 + delta + 128],
                                pt[gk % 4][:, 512 + delta:512 + delta + 128],
                                triM[:, :]), event=f"mask{p}_{gk}")
                        gkt += 1
                E.wait(f"normB{p}_7")
                E.do(f"otb{p % 2}", lambda e, p=p: e.dma_start(
                    out=OT[64:128, p * T:(p + 1) * T], in_=obt[p % 2][0:64, :]),
                    16, event=f"otbm{p}")

        def p3_post_dry():
            for p in range(8):
                ctx.events[f"pairdone{p}"] = ctx.events[f"otbm{p}"]
                gkt = 0
                for qc in range(NCH):
                    for kt in range(nkt(qc)):
                        if kt >= 4 * qc:
                            ctx.events[f"expm{p}_{gkt}"] = ctx.events[f"mask{p}_{gkt}"]
                        else:
                            ctx.events[f"expm{p}_{gkt}"] = ctx.events[f"exp{p}_{gkt}"]
                        ctx.events[f"ptfree{p}_{gkt}"] = ctx.events[f"pv{p}_{qc}_{kt}"]
                        gkt += 1


        p4 = contextlib.ExitStack()
        wo_b = p4.enter_context(nc.sbuf_tensor("wo_b", [128, 8 * D], BF16))
        wst4 = [p4.enter_context(nc.sbuf_tensor(f"wst4_{i}", [128, D], F32)) for i in range(2)]
        xb4 = [p4.enter_context(nc.sbuf_tensor(f"xb4_{i}", [128, D], F32)) for i in range(2)]
        gb4 = [p4.enter_context(nc.sbuf_tensor(f"gb4_{i}", [128, D], BF16)) for i in range(2)]
        gsil = [p4.enter_context(nc.sbuf_tensor(f"gsil{i}", [128, D], BF16)) for i in range(2)]
        tmp4 = p4.enter_context(nc.sbuf_tensor("tmp4", [128, D], F32))
        ob4 = [p4.enter_context(nc.sbuf_tensor(f"ob4_{i}", [128, D], F32)) for i in range(2)]

        def p4_sync_wo(E):
            for i in range(8):
                if i >= 2:
                    E.wait(f"wo_cast{i - 2}")
                E.do(f"wl{i % 2}", lambda e, i=i: e.dma_start(
                    out=wst4[i % 2][:, :],
                    in_=io["woT"].rearrange("(a p) n -> p a n", p=128)[:, i, :]),
                    16, event=f"wo_ld{i}")

        def p4_sync(E):
            for t in range(NTT):
                if t >= 2:
                    E.wait(f"comb{t - 2}")
                E.do(f"xg{t % 2}", lambda e, t=t: e.dma_start(
                    out=xb4[t % 2][:, :], in_=io["x"][t * 128:(t + 1) * 128, :]), 16)
                E.do(f"xg{t % 2}", lambda e, t=t: e.dma_start(
                    out=gb4[t % 2][:, :], in_=io["g_sp"][t * 128:(t + 1) * 128, :]),
                    16, event=f"g4_{t}")

        def p4_scalar_casts(E):
            for i in range(8):
                E.wait(f"wo_ld{i}")
                E.do("ACT", lambda e, i=i: e.activation(
                    wo_b[:, i * D:(i + 1) * D], wst4[i % 2][:, :], AF.Copy),
                    event=f"wo_cast{i}")

        def p4_scalar(E):
            for t in range(NTT):
                E.wait(f"g4_{t}")
                if t >= 2:
                    E.wait(f"m1_{t - 2}")
                E.do("ACT", lambda e, t=t: e.activation(
                    gsil[t % 2][:, :], gb4[t % 2][:, :], AF.Silu), event=f"sil{t}")
                E.wait(f"comb{t}")
                E.do(f"osp{t % 2}", lambda e, t=t: e.dma_start(
                    out=io["out"][t * 128:(t + 1) * 128, :], in_=ob4[t % 2][:, :]),
                    16, event=f"osp{t}")

        def p4_tensor(E):
            E.wait("wo_cast7", "exp7_143", "exp7_142", "rbscp7_7", "otbm6", "otbm7")
            for t in range(NTT):
                if t >= 2:
                    E.wait(f"m1_{t - 2}")
                for oc in range(2):
                    for p in range(8):
                        E.do("PE", lambda e, t=t, oc=oc, p=p: e.matmul(
                            pS4[:, (t % 2) * 1024 + oc * 512:(t % 2) * 1024 + (oc + 1) * 512],
                            OT[:, p * T + t * 128:p * T + (t + 1) * 128],
                            wo_b[:, p * D + oc * 512:p * D + (oc + 1) * 512],
                            start=(p == 0), stop=(p == 7)),
                            inc=1 if p == 7 else 0,
                            event=f"womm{t}_{oc}" if p == 7 else None)

        def p4_vec(E):
            for t in range(NTT):
                E.wait(f"womm{t}_0", f"womm{t}_1", f"sil{t}")
                E.do("DVE", lambda e, t=t: e.tensor_mul(
                    tmp4[:, :], pS4[:, (t % 2) * 1024:(t % 2) * 1024 + 1024],
                    gsil[t % 2][:, :]), event=f"m1_{t}")
                E.wait(f"g4_{t}", f"m1_{t}")
                if t >= 2:
                    E.wait(f"osp{t - 2}")
                E.do("DVE", lambda e, t=t: e.tensor_add(
                    ob4[t % 2][:, :], tmp4[:, :], xb4[t % 2][:, :]), event=f"comb{t}")

        def m_sync(E):
            p4_sync_wo(E)
            p3_sync(E)
            p4_sync(E)

        def m_tensor(E):
            p3_tensor(E)
            p4_tensor(E)

        def m_vector(E):
            p3_vec(E)
            p4_vec(E)

        def m_scalar(E):
            p4_scalar_casts(E)
            p3_scalar(E)
            p4_scalar(E)

        run_phase({"sync": m_sync, "tensor": m_tensor, "vector": m_vector,
                   "scalar": m_scalar, "gpsimd": p3_gp}, post_dry=p3_post_dry, name="P34")
        p4.close()
        p3.close()
        p34.close()

    return nc, ctx


def kernel(**inputs):
    in_maps = host_inputs(inputs)
    nc, _ = build()
    res = run_bass_kernel_spmd(nc, in_maps, core_ids=list(range(8)))
    return np.stack([np.asarray(res.results[c]["out"]) for c in range(8)], axis=0)


# version 12
# speedup vs baseline: 1.0793x; 1.0793x over previous
import contextlib
import numpy as np
import ml_dtypes
import concourse.bass as bass
import concourse.mybir as mybir
from concourse.bass_utils import run_bass_kernel_spmd

F32 = mybir.dt.float32
BF16 = mybir.dt.bfloat16
AF = mybir.ActivationFunctionType
ALU = mybir.AluOpType
AX = mybir.AxisListType
BF = ml_dtypes.bfloat16

T, D, HID, NH, DK = 4096, 1024, 4096, 16, 64
NCH, CH, NTT = 8, 512, 32
LN_EPS = 1e-5
ABS_EPS = 1e-6
EXP_BIAS = -12.0
ENGS = ("sync", "tensor", "vector", "scalar", "gpsimd")


class Ctx:
    def __init__(self):
        self.counts = {}
        self.emit_counts = {}
        self.events = {}
        self.seq = {}


class Em:
    def __init__(self, ctx, sems, eng, dry, engname=None):
        self.ctx, self.sems, self.eng, self.dry = ctx, sems, eng, dry
        self.engname = engname

    def do(self, sem_name, fn, inc=1, event=None):
        if self.dry:
            c = self.ctx.counts.get(sem_name, 0) + inc
            self.ctx.counts[sem_name] = c
            if inc:
                self.ctx.seq[self.engname].append(("inc", sem_name, inc))
            if event is not None:
                assert event not in self.ctx.events, f"dup event {event}"
                self.ctx.events[event] = (sem_name, c)
        else:
            ins = fn(self.eng)
            if inc:
                ins.then_inc(self.sems[sem_name], inc)
            c = self.ctx.emit_counts.get(sem_name, 0) + inc
            self.ctx.emit_counts[sem_name] = c
            if event is not None:
                assert self.ctx.events[event] == (sem_name, c), f"event drift {event}"

    def wait(self, *events):
        if self.dry:
            for ev in events:
                if ev is not None:
                    self.ctx.seq[self.engname].append(("wait", ev))
            return
        need = {}
        for ev in events:
            if ev is None:
                continue
            s, v = self.ctx.events[ev]
            need[s] = max(need.get(s, 0), v)
        for s, v in need.items():
            self.eng.wait_ge(self.sems[s], v)

    def wait_cnt(self, sem_name, v):
        if self.dry:
            self.ctx.seq[self.engname].append(("waitc", sem_name, v))
        else:
            self.eng.wait_ge(self.sems[sem_name], v)


def host_inputs(full):
    x = np.ascontiguousarray(np.asarray(full["L0_attn_x"], dtype=np.float32))
    Wq = np.asarray(full["L0_attn_Wq"], dtype=np.float32)
    Wk = np.asarray(full["L0_attn_Wk"], dtype=np.float32)
    Wv = np.asarray(full["L0_attn_Wv"], dtype=np.float32)
    Wo = np.asarray(full["L0_attn_Wo"], dtype=np.float32)
    Wfc = np.asarray(full["L0_ffn_W_fc"], dtype=np.float32)
    Wpr = np.asarray(full["L0_ffn_W_proj"], dtype=np.float32)
    lnw = np.asarray(full["L0_ffn_ln_w"], dtype=np.float32)
    lnb = np.asarray(full["L0_ffn_ln_b"], dtype=np.float32)

    perm_idx = np.concatenate(
        [np.concatenate([h * 64 + 2 * np.arange(32), h * 64 + 2 * np.arange(32) + 1])
         for h in range(NH)])
    const = {
        "wqT": np.ascontiguousarray(Wq[perm_idx, :].T),
        "wkT": np.ascontiguousarray(Wk[perm_idx, :].T),
        "wvT": np.ascontiguousarray(Wv.T),
        "woT": np.ascontiguousarray(Wo.T),
        "wfcT": np.ascontiguousarray(Wfc.T),
        "wprT": np.ascontiguousarray(Wpr.T),
        "lnw_col": np.ascontiguousarray(lnw.reshape(32, 128).T),
        "lnb_col": np.ascontiguousarray(lnb.reshape(32, 128).T),
    }
    j = np.arange(32, dtype=np.float64)
    inv = 10000.0 ** (-2.0 * j / 64.0)
    ang = np.arange(T, dtype=np.float64)[None, :] * inv[:, None]
    cos32 = np.cos(ang).astype(np.float32)
    sin32 = np.sin(ang).astype(np.float32)
    const["tabC"] = np.ascontiguousarray(np.tile(cos32, (4, 1)))
    const["tabS"] = np.ascontiguousarray(
        np.concatenate([-sin32, sin32, -sin32, sin32], axis=0))
    swap = np.arange(128)
    swap = np.where((swap % 64) < 32, swap + 32, swap - 32)
    permM = np.zeros((128, 128), np.float32)
    permM[np.arange(128), swap] = 1.0
    const["permM"] = np.asarray(permM.astype(BF))
    const["tri"] = np.asarray(np.triu(np.ones((128, 128), np.float32)).astype(BF))
    const["idb"] = np.asarray(np.eye(128, dtype=np.float32).astype(BF))
    const["idf"] = np.eye(128, dtype=np.float32)
    return [dict(const, x=np.ascontiguousarray(x[c])) for c in range(8)]


def nkt(qc):
    return 4 * qc + 4


def _check_deadlock(ctx, start_counts, name):
    sem = dict(start_counts)
    pos = {en: 0 for en in ctx.seq}
    seqs = ctx.seq
    progress = True
    while progress:
        progress = False
        for en, s in seqs.items():
            while pos[en] < len(s):
                op = s[pos[en]]
                if op[0] == "inc":
                    sem[op[1]] = sem.get(op[1], 0) + op[2]
                elif op[0] == "wait":
                    sm, v = ctx.events[op[1]]
                    if sem.get(sm, 0) < v:
                        break
                else:
                    if sem.get(op[1], 0) < op[2]:
                        break
                pos[en] += 1
                progress = True
    stuck = {en: (pos[en], len(s)) for en, s in seqs.items() if pos[en] < len(s)}
    if stuck:
        detail = {}
        for en in stuck:
            op = seqs[en][pos[en]]
            if op[0] == "wait":
                sm, v = ctx.events[op[1]]
                detail[en] = f"waits {op[1]} ({sm}>={v}, have {dict(sem).get(sm, 0)})"
            else:
                detail[en] = f"waits {op[1]}>={op[2]} (have {dict(sem).get(op[1], 0)})"
        raise RuntimeError(f"DEADLOCK in phase {name}: {stuck} :: {detail}")


NQB = 4


def build(dbg=False):
    import os
    PHASES = int(os.environ.get("PHASES", "9"))
    P1GAMMA = os.environ.get("P1GAMMA", "1") == "1"
    P1QKV = os.environ.get("P1QKV", "1") == "1"
    P1ROPE = os.environ.get("P1ROPE", "1") == "1"
    P1V = os.environ.get("P1V", "1") == "1"
    P1PERM = os.environ.get("P1PERM", "1") == "1"
    nc = bass.Bass("TRN2", target_bir_lowering=False)
    io = {}
    io["x"] = nc.dram_tensor("x", [T, D], F32, kind="ExternalInput")
    for n, sh, dt in [
        ("wqT", [D, D], F32), ("wkT", [D, D], F32), ("wvT", [D, D], F32),
        ("woT", [D, D], F32), ("wfcT", [D, HID], F32), ("wprT", [HID, D], F32),
        ("lnw_col", [128, 32], F32), ("lnb_col", [128, 32], F32),
        ("tabC", [128, T], F32), ("tabS", [128, T], F32),
        ("permM", [128, 128], BF16), ("tri", [128, 128], BF16),
        ("idb", [128, 128], BF16), ("idf", [128, 128], F32),
    ]:
        io[n] = nc.dram_tensor(n, sh, dt, kind="ExternalInput")
    io["out"] = nc.dram_tensor("out", [T, D], F32, kind="ExternalOutput")
    spk = "ExternalOutput" if dbg else "Internal"
    io["xn_sp"] = nc.dram_tensor("xn_sp", [D, T], BF16, kind=spk)
    io["qT_sp"] = nc.dram_tensor("qT_sp", [D, T], BF16, kind=spk)
    io["kT_sp"] = nc.dram_tensor("kT_sp", [D, T], BF16, kind=spk)
    io["v_sp"] = nc.dram_tensor("v_sp", [T, NH * 65], BF16, kind=spk)
    io["g_sp"] = nc.dram_tensor("g_sp", [T, D], BF16, kind=spk)

    ctx = Ctx()
    sems = {}
    with contextlib.ExitStack() as stack:
        stack.enter_context(nc.allow_low_precision(
            reason="bf16 tensor_tensor ops are 0/1-mask multiplies (exact) or h^2 at bf16 (within tolerance)"))
        QSEMS = {
            "sync": ["dIn", "cLNB", "cPRM", "tab0", "tab1", "tab2", "xb0", "xb1", "xb2", "xb3", "xb4", "wl0", "wl1",
                      "g2l0", "g2l1", "xnl", "pr0", "pr1", "xg0", "xg1"],
            "scalar": ["dOut", "xnsp0", "xnsp1", "qsp0", "qsp1", "ksp0", "ksp1", "vsp0", "vsp1",
                        "gspS", "osp0", "osp1"],
            "gpsimd": ["dGp", "gl0", "gl1", "otb0", "otb1"],
        }
        for sn in (["PE", "ACT", "DVE", "GP", "bar"] + QSEMS["sync"] + QSEMS["scalar"]
                   + QSEMS["gpsimd"]):
            sems[sn] = stack.enter_context(nc.semaphore(sn))

        konst = stack.enter_context(nc.sbuf_tensor("konst", [128, 8], F32))
        ones_b = stack.enter_context(nc.sbuf_tensor("ones_b", [128, 128], BF16))
        ones_f = stack.enter_context(nc.sbuf_tensor("ones_f", [128, 128], F32))
        idb = stack.enter_context(nc.sbuf_tensor("idb_s", [128, 128], BF16))
        idf = stack.enter_context(nc.sbuf_tensor("idf_s", [128, 128], F32))
        lnwc = stack.enter_context(nc.sbuf_tensor("lnw_s", [128, 32], F32))
        lnbc = stack.enter_context(nc.sbuf_tensor("lnb_s", [128, 32], F32))
        lnbc_bf = stack.enter_context(nc.sbuf_tensor("lnbb_s", [128, 32], BF16))
        thr = stack.enter_context(nc.sbuf_tensor("thr", [1, 8], F32))
        thrB = stack.enter_context(nc.sbuf_tensor("thrB", [128, 4], F32))

        barv = [0]

        def run_phase(builders, post_dry=None, name=""):
            start_counts = dict(ctx.counts)
            ctx.seq = {en: [] for en in ENGS}
            for en in ENGS:
                b = builders.get(en)
                if b:
                    b(Em(ctx, sems, None, True, engname=en))
            if post_dry:
                post_dry()
            _check_deadlock(ctx, start_counts, name)
            prev_bar = barv[0]
            barv[0] += len(ENGS)
            with nc.Block() as block:
                def mk(en, b):
                    def f(eng):
                        E = Em(ctx, sems, eng, False)
                        if prev_bar:
                            E.wait_cnt("bar", prev_bar)
                        if b:
                            b(E)
                        for sn in QSEMS.get(en, ()):
                            if ctx.counts.get(sn, 0):
                                E.wait_cnt(sn, ctx.counts[sn])
                        eng.nop().then_inc(sems["bar"], 1)
                    return f
                for en in ENGS:
                    getattr(block, en)(mk(en, builders.get(en)))

        def p0_sync(E):
            for dst, src in [(idb, "idb"), (idf, "idf"), (lnwc, "lnw_col")]:
                E.do("dIn", lambda e, dst=dst, src=src: e.dma_start(out=dst[:, :], in_=io[src][:, :]),
                     16, event=f"ld_{src}")
            E.do("cLNB", lambda e: e.dma_start(out=lnbc[:, :], in_=io["lnb_col"][:, :]),
                 16, event="ld_lnb_col")

        def p0_vec(E):
            E.do("DVE", lambda e: e.memset(konst[:, 1:2], EXP_BIAS))
            E.do("DVE", lambda e: e.memset(ones_b[:, :], 1.0))
            E.do("DVE", lambda e: e.memset(ones_f[:, :], 1.0))
            E.wait("ld_lnb_col")
            E.do("DVE", lambda e: e.tensor_copy(lnbc_bf[:, :], lnbc[:, :]), event="lnb_bf")

        run_phase({"sync": p0_sync, "vector": p0_vec}, name="P0")
        if PHASES < 1:
            return nc, ctx

        p1 = contextlib.ExitStack()
        wq_b = p1.enter_context(nc.sbuf_tensor("wq_b", [128, 8 * D], BF16))
        wk_b = p1.enter_context(nc.sbuf_tensor("wk_b", [128, 8 * D], BF16))
        wv_b = p1.enter_context(nc.sbuf_tensor("wv_b", [128, 8 * D], BF16))
        permM = p1.enter_context(nc.sbuf_tensor("perm_s", [128, 128], BF16))
        wst = [p1.enter_context(nc.sbuf_tensor(f"wst{i}", [128, D], F32)) for i in range(2)]
        gst = [p1.enter_context(nc.sbuf_tensor(f"gst{i}", [128, 2048], F32)) for i in range(2)]
        gcols = p1.enter_context(nc.sbuf_tensor("gcols", [128, 32], F32))
        tabC_s = [p1.enter_context(nc.sbuf_tensor(f"tabC{i}", [128, CH], F32)) for i in range(3)]
        tabS_s = [p1.enter_context(nc.sbuf_tensor(f"tabS{i}", [128, CH], F32)) for i in range(3)]
        xbuf = [p1.enter_context(nc.sbuf_tensor(f"xbuf{i}", [128, D], F32)) for i in range(4)]
        sq_scr = p1.enter_context(nc.sbuf_tensor("sq_scr", [128, 2 * D], F32))
        sc = p1.enter_context(nc.sbuf_tensor("sc", [128, 128], F32))
        xn_tok = [p1.enter_context(nc.sbuf_tensor(f"xn_tok{i}", [128, D], BF16)) for i in range(8)]
        xnT = [p1.enter_context(nc.sbuf_tensor(f"xnT{i}", [128, 8 * CH], BF16)) for i in range(2)]
        xqb = [p1.enter_context(nc.sbuf_tensor(f"xqb{i}", [128, CH], BF16)) for i in range(2)]
        tmpa = [p1.enter_context(nc.sbuf_tensor(f"tmpa{i}", [128, CH], F32)) for i in range(2)]
        tmpb = [p1.enter_context(nc.sbuf_tensor(f"tmpb{i}", [128, CH], F32)) for i in range(2)]
        qs = [p1.enter_context(nc.sbuf_tensor(f"qs{i}", [128, 8 * CH], BF16)) for i in range(2)]
        ks = [p1.enter_context(nc.sbuf_tensor(f"ks{i}", [128, 8 * CH], BF16)) for i in range(2)]
        vs = [p1.enter_context(nc.sbuf_tensor(f"vs{i}", [128, 4 * 1040], BF16)) for i in range(2)]
        pQ = [p1.enter_context(nc.psum_tensor(f"pQ{i}", [128, 512], F32)) for i in range(4)]
        pS = [p1.enter_context(nc.psum_tensor(f"pS{i}", [128, 512], F32)) for i in range(2)]
        pT = [p1.enter_context(nc.psum_tensor(f"pT{i}", [128, 1024], BF16)) for i in range(2)]

        WMATS = [("q", wq_b, "wqT", 0.125), ("k", wk_b, "wkT", 1.0), ("v", wv_b, "wvT", 1.0)]
        wnames = [(m, i) for m, _, _, _ in WMATS for i in range(8)]

        def p1_sync(E):
            if P1QKV:
                E.do("cPRM", lambda e: e.dma_start(out=permM[:, :], in_=io["permM"][:, :]), 16,
                     event="permld")
            for c in (range(3) if P1QKV else ()):
                E.do(f"tab{c % 3}", lambda e, c=c: e.dma_start(
                    out=tabC_s[c % 3][:, :], in_=io["tabC"][:, c * CH:(c + 1) * CH]), 16)
                E.do(f"tab{c % 3}", lambda e, c=c: e.dma_start(
                    out=tabS_s[c % 3][:, :], in_=io["tabS"][:, c * CH:(c + 1) * CH]),
                    16, event=f"tabS{c}")
            for g in range(4):
                E.do(f"xb{g % 4}", lambda e, g=g: e.dma_start(
                    out=xbuf[g % 4][:, :],
                    in_=io["x"].rearrange("(t p) m -> p t m", p=128)[:, g, :]),
                    16, event=f"xload{g}")
            for wi, (m, i) in enumerate(wnames if P1QKV else []):
                if wi >= 2:
                    pm, pi = wnames[wi - 2]
                    E.wait(f"wcast_{pm}{pi}")
                src = {"q": "wqT", "k": "wkT", "v": "wvT"}[m]
                E.do(f"wl{wi % 2}", lambda e, src=src, i=i, wi=wi: e.dma_start(
                    out=wst[wi % 2][:, :],
                    in_=io[src].rearrange("(a p) n -> p a n", p=128)[:, i, :]),
                    16, event=f"wload_{m}{i}")
                if wi == 7:
                    for g in (4, 5, 6, 7):
                        E.wait(f"apply{(g - 4) // 4}_{(g - 4) % 4}")
                        E.do(f"xb{g % 5}", lambda e, g=g: e.dma_start(
                            out=xbuf[g % 4][:, :],
                            in_=io["x"].rearrange("(t p) m -> p t m", p=128)[:, g, :]),
                            16, event=f"xload{g}")
            for c in range(0, NCH):
                for tt in range(4):
                    g = 4 * c + tt
                    if g < (8 if P1QKV else 4):
                        continue
                    E.wait(f"apply{(g - 4) // 4}_{(g - 4) % 4}")
                    E.do(f"xb{g % 5}", lambda e, g=g: e.dma_start(
                        out=xbuf[g % 4][:, :],
                        in_=io["x"].rearrange("(t p) m -> p t m", p=128)[:, g, :]),
                        16, event=f"xload{g}")
                c2 = c + 1
                if P1QKV and 3 <= c2 < NCH:
                    E.wait(f"rope{c2 - 3}k7")
                    E.do(f"tab{c2 % 3}", lambda e, c2=c2: e.dma_start(
                        out=tabC_s[c2 % 3][:, :], in_=io["tabC"][:, c2 * CH:(c2 + 1) * CH]), 16)
                    E.do(f"tab{c2 % 3}", lambda e, c2=c2: e.dma_start(
                        out=tabS_s[c2 % 3][:, :], in_=io["tabS"][:, c2 * CH:(c2 + 1) * CH]),
                        16, event=f"tabS{c2}")

        def p1_gp_rope_chunk(E, c):
            if not (P1QKV and P1ROPE):
                return
            for m in ("q", "k"):
                for i in range(8):
                    ri = c * 16 + (0 if m == "q" else 8) + i
                    if ri >= 2:
                        pc2, pm2, pi3 = ((c, m, i - 2) if i >= 2 else
                                         ((c, "q", i + 6) if m == "k" else (c - 1, "k", i + 6)))
                        E.wait(f"rope{pc2}{pm2}{pi3}")
                    E.wait(f"xq{c}{m}{i}", f"tabS{c}")
                    E.do("GP", lambda e, c=c, i=i, ri=ri: e.tensor_tensor(
                        tmpa[ri % 2][:, :], xqb[i % 2][:, :], tabC_s[c % 3][:, :], ALU.mult),
                        event=f"ra{c}{m}{i}")

        def p1_gp(E):
            for c in range(NCH):
                if c >= 1:
                    p1_gp_rope_chunk(E, c - 1)
                for j in (range(4 * c, 4 * c + 4) if P1GAMMA else ()):
                    if j >= 2:
                        E.wait(f"gred{j - 2}")
                    if j < 16:
                        E.do(f"gl{j % 2}", lambda e, j=j: e.dma_start(
                            out=gst[j % 2][:, :],
                            in_=io["wfcT"].rearrange("(a p) n -> p a n", p=128)
                            [:, j // 2, (j % 2) * 2048:(j % 2 + 1) * 2048]),
                            16, event=f"gload{j}")
                    else:
                        E.do(f"gl{j % 2}", lambda e, j=j: e.dma_start(
                            out=gst[j % 2][:, :].rearrange("p (a n) -> p a n", n=D),
                            in_=io["wprT"].rearrange("(a p) n -> p a n", p=128)
                            [:, 2 * (j - 16):2 * (j - 16) + 2, :]),
                            16, event=f"gload{j}")
            p1_gp_rope_chunk(E, NCH - 1)

        def p1_scalar(E):
            def emit_wcasts(lo, hi):
                for wi, (m, i) in list(enumerate(wnames if P1QKV else []))[lo:hi]:
                    dstb = {"q": wq_b, "k": wk_b, "v": wv_b}[m]
                    scale = 0.125 if m == "q" else 1.0
                    E.wait(f"wload_{m}{i}")
                    if scale == 1.0:
                        E.do("ACT", lambda e, dstb=dstb, i=i, wi=wi: e.activation(
                            dstb[:, i * D:(i + 1) * D], wst[wi % 2][:, :], AF.Copy),
                            event=f"wcast_{m}{i}")
                    else:
                        E.do("ACT", lambda e, dstb=dstb, i=i, wi=wi, scale=scale: e.activation(
                            dstb[:, i * D:(i + 1) * D], wst[wi % 2][:, :], AF.Copy, scale=scale),
                            event=f"wcast_{m}{i}")
            xq_rel = []

            def emit_smalls(c):
                b = (c % 2) * 32
                E.wait(f"veps{c}")
                E.do("ACT", lambda e, b=b: e.activation(
                    sc[:, b + 20:b + 24], sc[:, b + 20:b + 24], AF.Ln), event=f"lnx{c}")
                E.wait(f"lnx{c}")
                E.do("ACT", lambda e, b=b: e.activation(
                    sc[:, b + 24:b + 28], sc[:, b + 20:b + 24], AF.Exp, scale=-0.5),
                    event=f"rstd{c}")
                E.wait(f"nb{c}", f"rstd{c}")
                for tt in range(4):
                    g = c * 4 + tt
                    if g >= 8:
                        c2, tt2 = divmod(g - 8, 4)
                        E.wait(f"tr{c2}_{tt2}_7")
                    E.do("ACT", lambda e, c=c, tt=tt, b=b, g=g: e.activation(
                        xn_tok[g % 8][:, :], xbuf[g % 4][:, :], AF.Identity,
                        scale=sc[:, b + 24 + tt:b + 25 + tt], bias=sc[:, b + 28 + tt:b + 29 + tt]),
                        event=f"apply{c}_{tt}")

            def emit_copies(c):
                E.wait(f"trev{c}_3_7")
                E.do(f"xnsp{c % 2}", lambda e, c=c: e.dma_start(
                    out=io["xn_sp"].rearrange("(a p) t -> p a t", p=128)[:, :, c * CH:(c + 1) * CH],
                    in_=xnT[c % 2][:, :].rearrange("p (a t) -> p a t", t=CH)),
                    16, event=f"xnsp{c}")
                if not P1QKV:
                    return
                for mi, m in enumerate(("q", "k") if P1ROPE else ()):
                    for i in range(8):
                        u = c * 24 + mi * 8 + i
                        E.wait(f"pmm{c}{m}{i}")
                        if len(xq_rel) >= 2:
                            E.wait(*xq_rel[-2])
                        E.do("ACT", lambda e, i=i, u=u: e.activation(
                            xqb[i % 2][:, :], pQ[u % NQB][:, :], AF.Copy),
                            event=f"xq{c}{m}{i}")
                        xq_rel.append((f"perm{c}{m}{i}", f"ra{c}{m}{i}"))
                for uu in (range(8) if P1V else ()):
                    tt, vc = divmod(uu, 2)
                    u = c * 24 + 16 + uu
                    E.wait(f"vmm{c}_{tt}_{vc}")
                    E.do("ACT", lambda e, c=c, tt=tt, vc=vc, u=u: e.activation(
                        vs[c % 2][:, tt * 1040 + vc * 520:tt * 1040 + vc * 520 + 520]
                        .rearrange("p (h z) -> p h z", z=65)[:, :, 0:64],
                        pQ[u % NQB][:, :].rearrange("p (h z) -> p h z", z=64), AF.Copy),
                        event=f"vcp{c}_{tt}_{vc}")
                E.wait(f"rope{c}q7", f"rope{c}k7")
                E.do(f"qsp{c % 2}", lambda e, c=c: e.dma_start(
                    out=io["qT_sp"].rearrange("(a p) t -> p a t", p=128)[:, :, c * CH:(c + 1) * CH],
                    in_=qs[c % 2][:, :].rearrange("p (a t) -> p a t", t=CH)),
                    16, event=f"qsp{c}")
                E.do(f"ksp{c % 2}", lambda e, c=c: e.dma_start(
                    out=io["kT_sp"].rearrange("(a p) t -> p a t", p=128)[:, :, c * CH:(c + 1) * CH],
                    in_=ks[c % 2][:, :].rearrange("p (a t) -> p a t", t=CH)),
                    16, event=f"ksp{c}")
                if P1V:
                    E.wait(f"vcp{c}_3_1")
                    E.do(f"vsp{c % 2}", lambda e, c=c: e.dma_start(
                        out=io["v_sp"].rearrange("(ct p) n -> p ct n", p=128)[:, c * 4:(c + 1) * 4, :],
                        in_=vs[c % 2][:, :].rearrange("p (a n) -> p a n", n=1040)),
                        16, event=f"vsp{c}")

            emit_wcasts(0, 8)
            emit_smalls(0)
            emit_wcasts(8, 24)
            for c in range(1, NCH):
                emit_smalls(c)
                emit_copies(c - 1)
            emit_copies(NCH - 1)

        def p1_vec(E):
            for i in (range(2) if P1QKV else ()):
                E.do("DVE", lambda e, i=i: e.memset(vs[i][:, :], 1.0), event=f"vsones{i}")

            def emit_stats(c):
                b = (c % 2) * 32
                for tt in range(4):
                    g = c * 4 + tt
                    E.wait(f"xload{g}")
                    if g >= 1:
                        pg = g - 1
                        E.wait(f"stats{pg // 4}_{pg % 4}")
                    xt = xbuf[g % 4]
                    E.do("DVE", lambda e, xt=xt, g=g, b=b, tt=tt: e.tensor_scalar(
                        sq_scr[:, 0:D], xt[:, :], 1.0, 0.0, ALU.mult, ALU.add,
                        accum_out=sc[:, b + tt:b + tt + 1]))
                    E.do("DVE", lambda e, xt=xt, g=g, b=b, tt=tt: e.scalar_tensor_tensor(
                        sq_scr[:, D:2 * D], xt[:, :], 1.0, xt[:, :],
                        ALU.mult, ALU.mult, accum_out=sc[:, b + 4 + tt:b + 4 + tt + 1]),
                        event=f"stats{c}_{tt}")
                E.wait(f"stats{c}_3")
                E.do("DVE", lambda e, b=b: e.tensor_scalar(
                    sc[:, b + 8:b + 12], sc[:, b + 0:b + 4], 1.0 / D, None, ALU.mult),
                    event=f"mux{c}")
                E.do("DVE", lambda e, b=b: e.tensor_scalar(
                    sc[:, b + 12:b + 16], sc[:, b + 4:b + 8], 1.0 / D, None, ALU.mult),
                    event=f"m2x{c}")
                E.wait(f"mux{c}")
                E.do("DVE", lambda e, b=b: e.tensor_tensor(
                    sc[:, b + 16:b + 20], sc[:, b + 8:b + 12], sc[:, b + 8:b + 12], ALU.mult),
                    event=f"msq{c}")
                E.wait(f"m2x{c}", f"msq{c}")
                E.do("DVE", lambda e, b=b: e.scalar_tensor_tensor(
                    sc[:, b + 20:b + 24], sc[:, b + 12:b + 16], LN_EPS, sc[:, b + 16:b + 20],
                    ALU.add, ALU.subtract), event=f"veps{c}")

            def emit_nb(c):
                b = (c % 2) * 32
                E.wait(f"rstd{c}")
                E.do("DVE", lambda e, b=b: e.scalar_tensor_tensor(
                    sc[:, b + 28:b + 32], sc[:, b + 8:b + 12], -1.0, sc[:, b + 24:b + 28],
                    ALU.mult, ALU.mult), event=f"nb{c}")

            def emit_greds(c):
                for j in (range(4 * c, 4 * c + 4) if P1GAMMA else ()):
                    E.wait(f"gload{j}")
                    E.do("DVE", lambda e, j=j: e.tensor_reduce(
                        gcols[:, j:j + 1], gst[j % 2][:, :], AX.X, ALU.add,
                        apply_absolute_value=True), event=f"gred{j}")

            def emit_trevs(c):
                for tt in range(4):
                    for dt in range(8):
                        E.wait(f"tr{c}_{tt}_{dt}")
                        if c >= 2 and tt == 0 and dt == 0:
                            if P1QKV and P1V:
                                E.wait(f"vmm{c - 2}_3_1")
                            elif P1QKV:
                                E.wait(f"rope{c - 2}k7")
                            E.wait(f"xnsp{c - 2}")
                        E.do("DVE", lambda e, c=c, tt=tt, dt=dt: e.tensor_copy(
                            xnT[c % 2][:, dt * CH + tt * 128:dt * CH + (tt + 1) * 128],
                            pT[dt % 2][:, 0:128]), event=f"trev{c}_{tt}_{dt}")

            def emit_ropes(c):
                if not P1QKV:
                    return
                E.wait(f"tabS{c}")
                for m in ("q", "k"):
                    dst = qs if m == "q" else ks
                    for i in range(8):
                        ri = c * 16 + (0 if m == "q" else 8) + i
                        if not P1ROPE:
                            u = c * 24 + (0 if m == "q" else 8) + i
                            E.wait(f"pmm{c}{m}{i}")
                            if c >= 2:
                                E.wait(f"qsp{c - 2}" if m == "q" else f"ksp{c - 2}")
                            E.do("DVE", lambda e, dst=dst, c=c, i=i, u=u: e.tensor_copy(
                                dst[c % 2][:, i * CH:(i + 1) * CH], pQ[u % NQB][:, :]),
                                event=f"rope{c}{m}{i}")
                            if E.dry:
                                ctx.events.setdefault(f"ra{c}{m}{i}", ctx.events[f"rope{c}{m}{i}"])
                                ctx.events.setdefault(f"perm{c}{m}{i}", ctx.events[f"rope{c}{m}{i}"])
                                ctx.events.setdefault(f"rb{c}{m}{i}", ctx.events[f"rope{c}{m}{i}"])
                                ctx.events.setdefault(f"xq{c}{m}{i}", ctx.events[f"rope{c}{m}{i}"])
                            continue
                        E.wait(f"pmm{c}{m}{i}")
                        if ri >= 2:
                            pc2, pm2, pi3 = ((c, m, i - 2) if i >= 2 else
                                             ((c, "q", i + 6) if m == "k" else (c - 1, "k", i + 6)))
                            E.wait(f"rope{pc2}{pm2}{pi3}")
                        E.wait(f"perm{c}{m}{i}")
                        E.do("DVE", lambda e, c=c, i=i, ri=ri: e.tensor_tensor(
                            tmpb[ri % 2][:, :], pS[i % 2][:, :], tabS_s[c % 3][:, :], ALU.mult),
                            event=f"rb{c}{m}{i}")
                        E.wait(f"ra{c}{m}{i}", f"rb{c}{m}{i}")
                        if c >= 2:
                            E.wait(f"qsp{c - 2}" if m == "q" else f"ksp{c - 2}")
                        E.do("DVE", lambda e, dst=dst, c=c, i=i, ri=ri: e.tensor_add(
                            dst[c % 2][:, i * CH:(i + 1) * CH], tmpa[ri % 2][:, :], tmpb[ri % 2][:, :]),
                            event=f"rope{c}{m}{i}")

            for c in range(NCH):
                emit_stats(c)
                if c >= 1:
                    emit_trevs(c - 1)
                emit_nb(c)
                if c >= 1:
                    emit_ropes(c - 1)
                emit_greds(c)
            emit_trevs(NCH - 1)
            emit_ropes(NCH - 1)
            if not P1GAMMA:
                return
            E.wait("gfold_fc")
            E.do("DVE", lambda e: e.tensor_reduce(thr[0:1, 4:5], pS[0][0:1, 0:16], AX.X, ALU.add),
                 event="gtot_fc")
            E.wait("gtot_fc")
            E.do("DVE", lambda e: e.tensor_scalar(
                thr[0:1, 0:1], thr[0:1, 4:5], 0.5 / (D * HID), 0.5 * ABS_EPS,
                ALU.mult, ALU.add), event="thr_fc")
            E.wait("gfold_pr")
            E.do("DVE", lambda e: e.tensor_reduce(thr[0:1, 5:6], pS[0][0:1, 16:32], AX.X, ALU.add),
                 event="gtot_pr")
            E.wait("gtot_pr")
            E.do("DVE", lambda e: e.tensor_scalar(
                thr[0:1, 2:3], thr[0:1, 5:6], 0.5 / (D * HID), 0.5 * ABS_EPS,
                ALU.mult, ALU.add), event="thr_pr")
            E.wait("bc_fc")
            E.do("DVE", lambda e: e.tensor_copy(thrB[:, 0:1], pS[0][:, 40:41]))
            E.do("DVE", lambda e: e.tensor_scalar(
                thrB[:, 1:2], pS[0][:, 40:41], -1.0, None, ALU.mult), event="thrB_fc")
            E.wait("bc_pr")
            E.do("DVE", lambda e: e.tensor_copy(thrB[:, 2:3], pS[0][:, 41:42]))
            E.do("DVE", lambda e: e.tensor_scalar(
                thrB[:, 3:4], pS[0][:, 41:42], -1.0, None, ALU.mult), event="thrB_pr")

        def p1_tensor(E):
            grp = []

            def emit_tr(c):
                for tt in range(4):
                    E.wait(f"apply{c}_{tt}")
                    for dt in range(8):
                        gi = c * 32 + tt * 8 + dt
                        if gi >= 2:
                            pc, r = divmod(gi - 2, 32)
                            ptt, pdt = divmod(r, 8)
                            E.wait(f"trev{pc}_{ptt}_{pdt}")
                        E.do("PE", lambda e, c=c, tt=tt, dt=dt: e.transpose(
                            pT[dt % 2][:, 0:128], xn_tok[(c * 4 + tt) % 8][:, dt * 128:(dt + 1) * 128],
                            idb[:, :]), event=f"tr{c}_{tt}_{dt}")

            def emit_qkv(c):
                for m, wb, _, _ in (WMATS if P1QKV else []):
                    if m in ("q", "k"):
                        for i in range(8):
                            u = len(grp)
                            if u >= NQB:
                                E.wait(grp[u - NQB])
                            E.wait(f"wcast_{m}7", f"trev{c}_3_7")
                            for dt in range(8):
                                E.do("PE", lambda e, wb=wb, i=i, dt=dt, c=c, u=u: e.matmul(
                                    pQ[u % NQB][:, :], wb[:, dt * D + i * 128:dt * D + (i + 1) * 128],
                                    xnT[c % 2][:, dt * CH:(dt + 1) * CH],
                                    start=(dt == 0), stop=(dt == 7)),
                                    inc=1 if dt == 7 else 0,
                                    event=f"pmm{c}{m}{i}" if dt == 7 else None)
                            grp.append(f"xq{c}{m}{i}" if P1ROPE else f"ra{c}{m}{i}")
                            if not P1ROPE:
                                continue
                            if not P1PERM:
                                if E.dry:
                                    ctx.events[f"perm{c}{m}{i}"] = ctx.events[f"pmm{c}{m}{i}"]
                                continue
                            E.wait(f"xq{c}{m}{i}")
                            if not (c == 0 and m == "q" and i < 2):
                                pi = i - 2
                                pm, pc2 = m, c
                                if pi < 0:
                                    pm = "k" if m == "q" else "q"
                                    pc2 = c - 1 if m == "q" else c
                                    pi += 8
                                E.wait(f"rb{pc2}{pm}{pi}")
                            E.wait("permld")
                            E.do("PE", lambda e, i=i: e.matmul(
                                pS[i % 2][:, :], permM[:, :], xqb[i % 2][:, :],
                                start=True, stop=True), event=f"perm{c}{m}{i}")
                    else:
                        if not P1V:
                            continue
                        for uu in range(8):
                            tt, vc = divmod(uu, 2)
                            u = len(grp)
                            if u >= NQB:
                                E.wait(grp[u - NQB])
                            E.wait("wcast_v7")
                            if c < 2:
                                E.wait(f"vsones{c % 2}")
                            else:
                                E.wait(f"vsp{c - 2}")
                            for dt in range(8):
                                E.do("PE", lambda e, wb=wb, tt=tt, vc=vc, dt=dt, c=c, u=u: e.matmul(
                                    pQ[u % NQB][:, :],
                                    xnT[c % 2][:, dt * CH + tt * 128:dt * CH + (tt + 1) * 128],
                                    wb[:, dt * D + vc * 512:dt * D + (vc + 1) * 512],
                                    start=(dt == 0), stop=(dt == 7)),
                                    inc=1 if dt == 7 else 0,
                                    event=f"vmm{c}_{tt}_{vc}" if dt == 7 else None)
                            grp.append(f"vcp{c}_{tt}_{vc}")

            for c in range(NCH):
                if c >= 1:
                    emit_qkv(c - 1)
                emit_tr(c)
            emit_qkv(NCH - 1)
            if not P1GAMMA:
                return
            for j in range(16):
                E.wait(f"gred{j}")
            if P1QKV:
                E.wait(f"rb{NCH - 1}k6")
            E.do("PE", lambda e: e.matmul(pS[0][0:1, 0:16], ones_f[:, 0:1], gcols[:, 0:16],
                                          start=True, stop=True), event="gfold_fc")
            E.wait("gtot_fc")
            for j in range(16, 32):
                E.wait(f"gred{j}")
            E.do("PE", lambda e: e.matmul(pS[0][0:1, 16:32], ones_f[:, 0:1], gcols[:, 16:32],
                                          start=True, stop=True), event="gfold_pr")
            E.wait("gtot_pr", "thr_fc")
            E.do("PE", lambda e: e.matmul(pS[0][:, 40:41], ones_f[0:1, :], thr[0:1, 0:1],
                                          start=True, stop=True), event="bc_fc")
            E.wait("thr_pr", "thrB_fc")
            E.do("PE", lambda e: e.matmul(pS[0][:, 41:42], ones_f[0:1, :], thr[0:1, 2:3],
                                          start=True, stop=True), event="bc_pr")

        run_phase({"sync": p1_sync, "gpsimd": p1_gp, "scalar": p1_scalar,
                   "vector": p1_vec, "tensor": p1_tensor}, name="P1")
        p1.close()
        if PHASES < 2:
            return nc, ctx

        p2w = contextlib.ExitStack()
        wfc_q = p2w.enter_context(nc.sbuf_tensor("wfc_q", [128, 8 * HID], BF16))
        wpr_q = p2w.enter_context(nc.sbuf_tensor("wpr_q", [128, 32 * D], BF16))
        ubf = p2w.enter_context(nc.sbuf_tensor("ubf", [1, D], BF16))
        vbs = p2w.enter_context(nc.sbuf_tensor("vbs", [128, D], F32))

        p2a = contextlib.ExitStack()
        gst2 = [p2a.enter_context(nc.sbuf_tensor(f"gst2_{i}", [128, HID], F32)) for i in range(2)]
        qtmp = p2a.enter_context(nc.sbuf_tensor("qtmp", [128, HID], F32))
        vf = p2a.enter_context(nc.sbuf_tensor("vf", [1, D], F32))
        pV2 = [p2a.enter_context(nc.psum_tensor(f"pV2_{i}", [128, 512], F32)) for i in range(2)]
        pU2 = p2a.enter_context(nc.psum_tensor("pU2", [128, 1024], F32))

        def p2a_sync(E):
            for j in range(16):
                if j >= 2:
                    E.wait(f"qz{j - 2}")
                if j < 8:
                    E.do(f"g2l{j % 2}", lambda e, j=j: e.dma_start(
                        out=gst2[j % 2][:, :],
                        in_=io["wfcT"].rearrange("(a p) n -> p a n", p=128)[:, j, :]),
                        16, event=f"g2load{j}")
                else:
                    E.do(f"g2l{j % 2}", lambda e, j=j: e.dma_start(
                        out=gst2[j % 2][:, :].rearrange("p (a n) -> p a n", n=D),
                        in_=io["wprT"].rearrange("(a p) n -> p a n", p=128)[:, 4 * (j - 8):4 * (j - 7), :]),
                        16, event=f"g2load{j}")

        def p2a_vec(E):
            for j in range(16):
                E.wait(f"g2load{j}", "thrB_fc" if j < 8 else "thrB_pr", "lnb_bf")
                if j >= 1:
                    E.wait(f"qz{j - 1}")
                toff = 0 if j < 8 else 2
                E.do("DVE", lambda e, j=j, toff=toff: e.tensor_scalar(
                    qtmp[:, :], gst2[j % 2][:, :], thrB[:, toff + 1:toff + 2], None, ALU.is_le),
                    event=f"qt{j}")
                E.wait(f"qt{j}")
                if j < 8:
                    E.do("DVE", lambda e, j=j, toff=toff: e.scalar_tensor_tensor(
                        wfc_q[:, j * HID:(j + 1) * HID], gst2[j % 2][:, :],
                        thrB[:, toff:toff + 1], qtmp[:, :], ALU.is_ge, ALU.subtract),
                        event=f"qz{j}")
                else:
                    E.do("DVE", lambda e, j=j, toff=toff: e.scalar_tensor_tensor(
                        wpr_q[:, (j - 8) * 4 * D:(j - 7) * 4 * D], gst2[j % 2][:, :],
                        thrB[:, toff:toff + 1], qtmp[:, :], ALU.is_ge, ALU.subtract),
                        event=f"qz{j}")
            for jt in range(32):
                E.wait(f"vmm2_{jt}")
                E.do("DVE", lambda e, jt=jt: e.tensor_scalar(
                    wpr_q[:, jt * D:(jt + 1) * D], wpr_q[:, jt * D:(jt + 1) * D],
                    lnwc[:, jt:jt + 1], None, ALU.mult), event=f"wsc{jt}")
            E.wait("vbs_mm")
            E.do("DVE", lambda e: e.tensor_copy(vbs[:, 0:512], pV2[0][:, 0:512]))
            E.do("DVE", lambda e: e.tensor_copy(vbs[:, 512:D], pV2[1][:, 0:512]), event="vbs")

        def p2a_scalar(E):
            E.wait("uv_v")
            E.do("ACT", lambda e: e.activation(vf[0:1, 0:512], pV2[0][0:1, 0:512], AF.Copy))
            E.do("ACT", lambda e: e.activation(vf[0:1, 512:D], pV2[1][0:1, 0:512], AF.Copy), event="vf")
            E.wait("uv_u")
            E.do("ACT", lambda e: e.activation(ubf[0:1, :], pU2[0:1, 0:1024], AF.Copy), event="ubf")

        def p2a_tensor(E):
            for jt in range(32):
                E.wait(f"qz{8 + jt // 4}")
                for half in range(2):
                    E.do("PE", lambda e, jt=jt, half=half: e.matmul(
                        pV2[half][0:1, 0:512], lnbc_bf[:, jt:jt + 1],
                        wpr_q[:, jt * D + half * 512:jt * D + (half + 1) * 512],
                        start=(jt == 0), stop=(jt == 31)),
                        inc=1 if half == 1 else 0,
                        event=f"vmm2_{jt}" if half == 1 else None)
            for jt in range(32):
                E.wait(f"wsc{jt}")
                for half in range(2):
                    E.do("PE", lambda e, jt=jt, half=half: e.matmul(
                        pU2[0:1, half * 512:(half + 1) * 512], ones_b[:, 0:1],
                        wpr_q[:, jt * D + half * 512:jt * D + (half + 1) * 512],
                        start=(jt == 0), stop=(jt == 31)),
                        inc=1 if (jt == 31 and half == 1) else 0,
                        event="uv_u" if (jt == 31 and half == 1) else None)
            E.wait("vf")
            E.do("PE", lambda e: e.matmul(
                pV2[0][:, 0:512], ones_f[0:1, :], vf[0:1, 0:512], start=True, stop=True), inc=0)
            E.do("PE", lambda e: e.matmul(
                pV2[1][:, 0:512], ones_f[0:1, :], vf[0:1, 512:D], start=True, stop=True),
                event="vbs_mm")

        def p2a_post_dry():
            ctx.events["uv_v"] = ctx.events["vmm2_31"]

        run_phase({"sync": p2a_sync, "vector": p2a_vec, "scalar": p2a_scalar,
                   "tensor": p2a_tensor}, post_dry=p2a_post_dry, name="P2a")
        p2a.close()
        if PHASES < 3:
            return nc, ctx

        p2 = contextlib.ExitStack()
        xnf = p2.enter_context(nc.sbuf_tensor("xnf", [128, 8 * CH], BF16))
        hbuf = p2.enter_context(nc.sbuf_tensor("hbuf", [128, 32 * CH], BF16))
        hsq = [p2.enter_context(nc.sbuf_tensor(f"hsq{i}", [128, CH], BF16)) for i in range(4)]
        sst = p2.enter_context(nc.sbuf_tensor("sst", [128, 5 * CH], F32))
        sstb = p2.enter_context(nc.sbuf_tensor("sstb", [1, CH], BF16))
        ist = p2.enter_context(nc.sbuf_tensor("ist", [128, 8], F32))
        gbuf = p2.enter_context(nc.sbuf_tensor("gbuf", [128, 4 * D], BF16))
        pH = [p2.enter_context(nc.psum_tensor(f"pH{i}", [128, 512], F32)) for i in range(2)]
        pStats = p2.enter_context(nc.psum_tensor("pStats", [128, 1024], F32))
        pP = [p2.enter_context(nc.psum_tensor(f"pP{i}", [128, 512], F32)) for i in range(2)]
        pIst = p2.enter_context(nc.psum_tensor("pIst", [128, 512], F32))

        S_MU, S_M2, S_VEPS, S_LN, S_RSTD = 0, CH, 2 * CH, 3 * CH, 4 * CH

        def p2_sync(E):
            for c in range(NCH):
                if c >= 1:
                    E.wait(f"hmm{c - 1}_31")
                E.do("xnl", lambda e, c=c: e.dma_start(
                    out=xnf[:, :].rearrange("p (a t) -> p a t", t=CH),
                    in_=io["xn_sp"].rearrange("(a p) t -> p a t", p=128)[:, :, c * CH:(c + 1) * CH]),
                    16, event=f"xnload{c}")

        def p2_vec(E):
            E.do("DVE", lambda e: e.memset(sst[:, :], 0.0), event="sstz")
            for c in range(NCH):
                for jt in range(32):
                    E.wait(f"gelu{c}_{jt}")
                    if c * 32 + jt >= 4:
                        pj = c * 32 + jt - 4
                        E.wait(f"ssq{pj // 32}_{pj % 32}")
                    E.do("DVE", lambda e, c=c, jt=jt: e.tensor_tensor(
                        hsq[jt % 4][:, :], hbuf[:, jt * CH:(jt + 1) * CH],
                        hbuf[:, jt * CH:(jt + 1) * CH], ALU.mult), event=f"hsq{c}_{jt}")
                E.wait(f"smm{c}")
                if c >= 1:
                    E.wait(f"nmu{c - 1}")
                E.do("DVE", lambda e: e.tensor_scalar(
                    sst[0:1, S_MU:S_MU + CH], pStats[0:1, 0:512], 1.0 / HID, None, ALU.mult),
                    event=f"fmu_{c}")
                E.do("DVE", lambda e: e.tensor_scalar(
                    sst[0:1, S_M2:S_M2 + CH], pStats[0:1, 512:1024], 1.0 / HID, None, ALU.mult),
                    event=f"m2_{c}")
                E.wait(f"fmu_{c}")
                E.do("DVE", lambda e: e.tensor_tensor(
                    sst[0:1, S_VEPS:S_VEPS + CH], sst[0:1, S_MU:S_MU + CH],
                    sst[0:1, S_MU:S_MU + CH], ALU.mult), event=f"fmsq_{c}")
                E.wait(f"m2_{c}", f"fmsq_{c}")
                E.do("DVE", lambda e: e.scalar_tensor_tensor(
                    sst[0:1, S_VEPS:S_VEPS + CH], sst[0:1, S_M2:S_M2 + CH], LN_EPS,
                    sst[0:1, S_VEPS:S_VEPS + CH], ALU.add, ALU.subtract), event=f"fveps_{c}")
                E.wait(f"frstd_{c}")
                if c >= 1:
                    E.wait(f"pmm2_{c - 1}_3_1")
                E.do("DVE", lambda e: e.tensor_scalar(
                    sstb[0:1, :], sst[0:1, S_MU:S_MU + CH], -1.0, None, ALU.mult),
                    event=f"nmu{c}")
                for tt in range(4):
                    E.wait(f"istT{c}_{tt}")
                    E.do("DVE", lambda e, c=c, tt=tt: e.tensor_copy(
                        ist[:, (c % 2) * 4 + tt:(c % 2) * 4 + tt + 1], pIst[:, 0:1]),
                        event=f"ist{c}_{tt}")
                for tsub in range(4):
                    for ddc in range(2):
                        E.wait(f"pmm2_{c}_{tsub}_{ddc}", f"ist{c}_{tsub}")
                        if c >= 1:
                            E.wait(f"gsp{c - 1}")
                        E.do("DVE", lambda e, c=c, tsub=tsub, ddc=ddc: e.scalar_tensor_tensor(
                            gbuf[:, tsub * D + ddc * 512:tsub * D + (ddc + 1) * 512],
                            pP[(tsub * 2 + ddc) % 2][:, 0:512],
                            ist[:, (c % 2) * 4 + tsub:(c % 2) * 4 + tsub + 1],
                            vbs[:, ddc * 512:(ddc + 1) * 512], ALU.mult, ALU.add),
                            event=f"gevac{c}_{tsub}_{ddc}")

        def p2_scalar(E):
            for c in range(NCH):
                for jt in range(32):
                    E.wait(f"hmm{c}_{jt}")
                    if c >= 1 and jt == 0:
                        E.wait(f"pmm2_{c - 1}_3_1")
                    E.do("ACT", lambda e, c=c, jt=jt: e.activation(
                        hbuf[:, jt * CH:(jt + 1) * CH], pH[jt % 2][:, :], AF.Gelu),
                        event=f"gelu{c}_{jt}")
                E.wait(f"fveps_{c}")
                E.do("ACT", lambda e: e.activation(
                    sst[0:1, S_LN:S_LN + CH], sst[0:1, S_VEPS:S_VEPS + CH], AF.Ln),
                    event=f"fln_{c}")
                E.wait(f"fln_{c}")
                E.do("ACT", lambda e: e.activation(
                    sst[0:1, S_RSTD:S_RSTD + CH], sst[0:1, S_LN:S_LN + CH], AF.Exp, scale=-0.5),
                    event=f"frstd_{c}")
                for tsub in range(4):
                    for ddc in range(2):
                        E.wait(f"gevac{c}_{tsub}_{ddc}")
                E.do("gspS", lambda e, c=c: e.dma_start(
                    out=io["g_sp"].rearrange("(ct p) n -> p ct n", p=128)[:, c * 4:(c + 1) * 4, :],
                    in_=gbuf[:, :].rearrange("p (a n) -> p a n", n=D)),
                    16, event=f"gsp{c}")

        def p2_tensor(E):
            for c in range(NCH):
                E.wait(f"xnload{c}")
                for jt in range(32):
                    gi = c * 32 + jt
                    if gi >= 2:
                        pj = gi - 2
                        E.wait(f"gelu{pj // 32}_{pj % 32}")
                    for dt in range(8):
                        E.do("PE", lambda e, c=c, jt=jt, dt=dt: e.matmul(
                            pH[jt % 2][:, :], wfc_q[:, dt * HID + jt * 128:dt * HID + (jt + 1) * 128],
                            xnf[:, dt * CH:(dt + 1) * CH],
                            start=(dt == 0), stop=(dt == 7)),
                            inc=1 if dt == 7 else 0, event=f"hmm{c}_{jt}" if dt == 7 else None)
                for jt in range(32):
                    E.wait(f"gelu{c}_{jt}")
                    if c >= 1 and jt == 0:
                        E.wait(f"m2_{c - 1}")
                    E.do("PE", lambda e, c=c, jt=jt: e.matmul(
                        pStats[0:1, 0:512], ones_b[:, 0:1], hbuf[:, jt * CH:(jt + 1) * CH],
                        start=(jt == 0), stop=(jt == 31)), inc=0)
                    E.wait(f"hsq{c}_{jt}")
                    E.do("PE", lambda e, c=c, jt=jt: e.matmul(
                        pStats[0:1, 512:1024], ones_b[:, 0:1], hsq[jt % 4][:, :],
                        start=(jt == 0), stop=(jt == 31)),
                        event=f"ssq{c}_{jt}")
                for tt in range(4):
                    E.wait(f"frstd_{c}")
                    gi = c * 4 + tt
                    if gi >= 1:
                        pv = gi - 1
                        E.wait(f"ist{pv // 4}_{pv % 4}")
                    E.do("PE", lambda e, tt=tt: e.transpose(
                        pIst[:, 0:128], sst[0:128, S_RSTD + tt * 128:S_RSTD + (tt + 1) * 128],
                        idf[:, :]), event=f"istT{c}_{tt}")
                for tsub in range(4):
                    for ddc in range(2):
                        u = tsub * 2 + ddc
                        gi = c * 8 + u
                        if gi >= 2:
                            pu = gi - 2
                            E.wait(f"gevac{pu // 8}_{(pu % 8) // 2}_{pu % 2}")
                        E.wait(f"nmu{c}")
                        for jt in range(32):
                            E.do("PE", lambda e, c=c, tsub=tsub, ddc=ddc, jt=jt: e.matmul(
                                pP[(tsub * 2 + ddc) % 2][:, 0:512],
                                hbuf[:, jt * CH + tsub * 128:jt * CH + (tsub + 1) * 128],
                                wpr_q[:, jt * D + ddc * 512:jt * D + (ddc + 1) * 512],
                                start=(jt == 0), stop=False), inc=0)
                        E.do("PE", lambda e, c=c, tsub=tsub, ddc=ddc: e.matmul(
                            pP[(tsub * 2 + ddc) % 2][:, 0:512],
                            sstb[0:1, tsub * 128:(tsub + 1) * 128],
                            ubf[0:1, ddc * 512:(ddc + 1) * 512],
                            start=False, stop=True),
                            event=f"pmm2_{c}_{tsub}_{ddc}")

        def p2_post_dry():
            for c in range(NCH):
                ctx.events[f"smm{c}"] = ctx.events[f"ssq{c}_31"]

        run_phase({"sync": p2_sync, "scalar": p2_scalar, "vector": p2_vec,
                   "tensor": p2_tensor}, post_dry=p2_post_dry, name="P2b")
        p2.close()
        p2w.close()
        if PHASES < 4:
            return nc, ctx

        p34 = contextlib.ExitStack()
        OT = p34.enter_context(nc.sbuf_tensor("OT", [128, 8 * T], BF16))
        p3 = contextlib.ExitStack()
        triM = p3.enter_context(nc.sbuf_tensor("tri_s", [128, 128], BF16))
        qp = [p3.enter_context(nc.sbuf_tensor(f"qp{i}", [128, T], BF16)) for i in range(2)]
        kp = [p3.enter_context(nc.sbuf_tensor(f"kp{i}", [128, T], BF16)) for i in range(2)]
        vp = [p3.enter_context(nc.sbuf_tensor(f"vp{i}", [128, 32 * 130], BF16)) for i in range(2)]
        pt = [p3.enter_context(nc.sbuf_tensor(f"pt{i}", [128, 1024], BF16)) for i in range(4)]
        rcp = p3.enter_context(nc.sbuf_tensor("rcp", [128, 1024], BF16))
        lnt = p3.enter_context(nc.sbuf_tensor("lnt", [128, 512], F32))
        rbs = p3.enter_context(nc.sbuf_tensor("rbs", [128, 1024], F32))
        obt = [p3.enter_context(nc.sbuf_tensor(f"obt{i}", [128, T], BF16)) for i in range(2)]
        pS4 = p3.enter_context(nc.psum_tensor("pS4", [128, 2048], F32))
        pOAB = [p3.enter_context(nc.psum_tensor(f"pOAB{i}", [128, 1024], F32)) for i in range(2)]

        def p3_sync(E):
            E.do("cPRM", lambda e: e.dma_start(out=triM[:, :], in_=io["tri"][:, :]), 16,
                 event="trild")
            for p in range(8):
                if p >= 2:
                    E.wait(f"pairdone{p - 2}")
                E.do(f"pr{p % 2}", lambda e, p=p: e.dma_start(
                    out=qp[p % 2][:, :], in_=io["qT_sp"][p * 128:(p + 1) * 128, :]), 16)
                E.do(f"pr{p % 2}", lambda e, p=p: e.dma_start(
                    out=kp[p % 2][:, :], in_=io["kT_sp"][p * 128:(p + 1) * 128, :]), 16)
                E.do(f"pr{p % 2}", lambda e, p=p: e.dma_start(
                    out=vp[p % 2][:, :].rearrange("p (kt z) -> p kt z", z=130),
                    in_=io["v_sp"].rearrange("(kt p) n -> p kt n", p=128)[:, :, p * 130:(p + 1) * 130]),
                    16, event=f"vpl{p}")

        def p3_tensor(E):
            pending_rb = [None]
            last_rb_pair = {0: None, 1: None}

            def emit_rb(E, cur_p, cur_g0):
                if pending_rb[0] is None:
                    return
                p, qc, rbp = pending_rb[0]
                pending_rb[0] = None
                E.wait(f"recip{p}_{qc}", f"recipB{p}_{qc}")
                if cur_g0 is not None:
                    E.wait(f"exp{cur_p}_{cur_g0 + 1}")
                if not (p == 0 and qc == 0):
                    E.wait(f"rbscp{p if qc > 0 else p - 1}_{qc - 1 if qc > 0 else 7}")
                E.do("PE", lambda e, rbp=rbp: e.matmul(
                    pS4[0:64, rbp * 1024:rbp * 1024 + 512], ones_b[64:65, 0:64],
                    rcp[64:65, 0:512],
                    start=True, stop=True, tile_position=(64, 0)), inc=0)
                E.do("PE", lambda e, rbp=rbp: e.matmul(
                    pS4[0:64, rbp * 1024 + 512:rbp * 1024 + 1024], ones_b[64:65, 0:64],
                    rcp[64:65, 512:1024],
                    start=True, stop=True, tile_position=(64, 0)),
                    event=f"rbmm{p}_{qc}")
                last_rb_pair[rbp] = f"rbscp{p}_{qc}"

            for p in range(8):
                E.wait(f"vpl{p}")
                gkt = 0
                for qc in range(NCH):
                    g0 = gkt
                    n = nkt(qc)

                    def do_s(kt, p=p, qc=qc, g0=g0):
                        gk = g0 + kt
                        qtr = gk % 2
                        if gk >= 2:
                            E.wait(f"exp{p}_{gk - 2}")
                        if last_rb_pair[qtr] is not None:
                            E.wait(last_rb_pair[qtr])
                            last_rb_pair[qtr] = None
                        E.do("PE", lambda e, kt=kt, qtr=qtr: e.matmul(
                            pS4[:, qtr * 1024:qtr * 1024 + 512],
                            kp[p % 2][0:64, kt * 128:(kt + 1) * 128],
                            qp[p % 2][0:64, qc * CH:(qc + 1) * CH],
                            start=True, stop=True, tile_position=(0, 0)), inc=0)
                        E.do("PE", lambda e, kt=kt, qtr=qtr: e.matmul(
                            pS4[:, qtr * 1024 + 512:qtr * 1024 + 1024],
                            kp[p % 2][64:128, kt * 128:(kt + 1) * 128],
                            qp[p % 2][64:128, qc * CH:(qc + 1) * CH],
                            start=True, stop=True, tile_position=(64, 0)),
                            event=f"smm{p}_{gk}")

                    def do_pv(kt, p=p, qc=qc, g0=g0, n=n):
                        gk = g0 + kt
                        delta = max(0, kt * 128 - qc * CH)
                        E.wait(f"expm{p}_{gk}")
                        if kt == 0:
                            gq = p * NCH + qc
                            if gq >= 2:
                                pqc = gq - 2
                                E.wait(f"normA{pqc // NCH}_{pqc % NCH}",
                                       f"normB{pqc // NCH}_{pqc % NCH}")
                        E.do("PE", lambda e, kt=kt, delta=delta, qc=qc: e.matmul(
                            pOAB[qc % 2][0:65, delta:512], vp[p % 2][:, kt * 130:kt * 130 + 65],
                            pt[(g0 + kt) % 4][:, delta:512],
                            start=(kt == 0), stop=(kt == n - 1)), inc=0)
                        E.do("PE", lambda e, kt=kt, delta=delta, qc=qc: e.matmul(
                            pOAB[qc % 2][0:65, 512 + delta:1024],
                            vp[p % 2][:, kt * 130 + 65:kt * 130 + 130],
                            pt[(g0 + kt) % 4][:, 512 + delta:1024],
                            start=(kt == 0), stop=(kt == n - 1)),
                            event=f"pv{p}_{qc}_{kt}")

                    for kt in range(n):
                        do_s(kt)
                        if kt == 2:
                            emit_rb(E, p, g0)
                        if kt >= 2:
                            do_pv(kt - 2)
                    do_pv(n - 2)
                    do_pv(n - 1)
                    gkt += n
                    pending_rb[0] = (p, qc, (gkt + 1) % 2)
            emit_rb(E, None, None)

        def p3_scalar(E):
            for p in range(8):
                gkt = 0
                for qc in range(NCH):
                    for kt in range(nkt(qc)):
                        E.wait(f"smm{p}_{gkt}")
                        if gkt >= 4:
                            E.wait(f"ptfree{p}_{gkt - 4}")
                        E.do("ACT", lambda e, p=p, gkt=gkt: e.activation(
                            pt[gkt % 4][:, :], pS4[:, (gkt % 2) * 1024:(gkt % 2) * 1024 + 1024],
                            AF.Exp, bias=konst[:, 1:2]),
                            event=f"exp{p}_{gkt}")
                        gkt += 1
                    E.wait(f"pv{p}_{qc}_{nkt(qc) - 1}")
                    if not (p == 0 and qc == 0):
                        E.wait(f"rbmm{p if qc > 0 else p - 1}_{qc - 1 if qc > 0 else 7}")
                    E.do("ACT", lambda e, qc=qc: e.activation(
                        lnt[64:65, 0:512], pOAB[qc % 2][64:65, 512:1024], AF.Ln), inc=0)
                    E.do("ACT", lambda e: e.activation(
                        rcp[64:65, 512:1024], lnt[64:65, 0:512], AF.Exp, scale=-1.0),
                        event=f"recipB{p}_{qc}")

        def p3_vec(E):
            for p in range(8):
                gkt = 0
                for qc in range(NCH):
                    gkt += nkt(qc)
                    rbp = (gkt + 1) % 2
                    E.wait(f"pv{p}_{qc}_{nkt(qc) - 1}")
                    if not (p == 0 and qc == 0):
                        E.wait(f"rbmm{p if qc > 0 else p - 1}_{qc - 1 if qc > 0 else 7}")
                    E.do("DVE", lambda e, qc=qc: e.reciprocal(
                        rcp[64:65, 0:512], pOAB[qc % 2][64:65, 0:512]),
                         event=f"recip{p}_{qc}")
                    E.wait(f"rbmm{p}_{qc}")
                    E.do("DVE", lambda e, rbp=rbp: e.tensor_copy(
                        rbs[0:64, 0:1024], pS4[0:64, rbp * 1024:rbp * 1024 + 1024]),
                        event=f"rbscp{p}_{qc}")
                    E.wait(f"rbscp{p}_{qc}")
                    E.do("DVE", lambda e, p=p, qc=qc: e.tensor_mul(
                        OT[0:64, p * T + qc * CH:p * T + (qc + 1) * CH],
                        pOAB[qc % 2][0:64, 0:512], rbs[0:64, 0:512]), event=f"normA{p}_{qc}")
                    if p >= 2 and qc == 0:
                        E.wait(f"otbm{p - 2}")
                    E.do("DVE", lambda e, p=p, qc=qc: e.tensor_mul(
                        obt[p % 2][0:64, qc * CH:(qc + 1) * CH],
                        pOAB[qc % 2][0:64, 512:1024], rbs[0:64, 512:1024]), event=f"normB{p}_{qc}")

        def p3_gp(E):
            for p in range(8):
                gkt = 0
                for qc in range(NCH):
                    for kt in range(nkt(qc)):
                        gk = gkt
                        delta = max(0, kt * 128 - qc * CH)
                        if kt >= 4 * qc:
                            E.wait(f"exp{p}_{gk}", "trild")
                            E.do("GP", lambda e, gk=gk, delta=delta: e.tensor_mul(
                                pt[gk % 4][:, delta:delta + 128], pt[gk % 4][:, delta:delta + 128],
                                triM[:, :]), inc=0)
                            E.do("GP", lambda e, gk=gk, delta=delta: e.tensor_mul(
                                pt[gk % 4][:, 512 + delta:512 + delta + 128],
                                pt[gk % 4][:, 512 + delta:512 + delta + 128],
                                triM[:, :]), event=f"mask{p}_{gk}")
                        gkt += 1
                E.wait(f"normB{p}_7")
                E.do(f"otb{p % 2}", lambda e, p=p: e.dma_start(
                    out=OT[64:128, p * T:(p + 1) * T], in_=obt[p % 2][0:64, :]),
                    16, event=f"otbm{p}")

        def p3_post_dry():
            for p in range(8):
                ctx.events[f"pairdone{p}"] = ctx.events[f"otbm{p}"]
                gkt = 0
                for qc in range(NCH):
                    for kt in range(nkt(qc)):
                        if kt >= 4 * qc:
                            ctx.events[f"expm{p}_{gkt}"] = ctx.events[f"mask{p}_{gkt}"]
                        else:
                            ctx.events[f"expm{p}_{gkt}"] = ctx.events[f"exp{p}_{gkt}"]
                        ctx.events[f"ptfree{p}_{gkt}"] = ctx.events[f"pv{p}_{qc}_{kt}"]
                        gkt += 1


        p4 = contextlib.ExitStack()
        wo_b = p4.enter_context(nc.sbuf_tensor("wo_b", [128, 8 * D], BF16))
        wst4 = [p4.enter_context(nc.sbuf_tensor(f"wst4_{i}", [128, D], F32)) for i in range(2)]
        xb4 = [p4.enter_context(nc.sbuf_tensor(f"xb4_{i}", [128, D], F32)) for i in range(2)]
        gb4 = [p4.enter_context(nc.sbuf_tensor(f"gb4_{i}", [128, D], BF16)) for i in range(2)]
        gsil = [p4.enter_context(nc.sbuf_tensor(f"gsil{i}", [128, D], BF16)) for i in range(2)]
        tmp4 = p4.enter_context(nc.sbuf_tensor("tmp4", [128, D], F32))
        ob4 = [p4.enter_context(nc.sbuf_tensor(f"ob4_{i}", [128, D], F32)) for i in range(2)]

        def p4_sync_wo(E):
            for i in range(8):
                if i >= 2:
                    E.wait(f"wo_cast{i - 2}")
                E.do(f"wl{i % 2}", lambda e, i=i: e.dma_start(
                    out=wst4[i % 2][:, :],
                    in_=io["woT"].rearrange("(a p) n -> p a n", p=128)[:, i, :]),
                    16, event=f"wo_ld{i}")

        def p4_sync(E):
            for t in range(NTT):
                if t >= 2:
                    E.wait(f"comb{t - 2}")
                E.do(f"xg{t % 2}", lambda e, t=t: e.dma_start(
                    out=xb4[t % 2][:, :], in_=io["x"][t * 128:(t + 1) * 128, :]), 16)
                E.do(f"xg{t % 2}", lambda e, t=t: e.dma_start(
                    out=gb4[t % 2][:, :], in_=io["g_sp"][t * 128:(t + 1) * 128, :]),
                    16, event=f"g4_{t}")

        def p4_scalar_casts(E):
            for i in range(8):
                E.wait(f"wo_ld{i}")
                E.do("ACT", lambda e, i=i: e.activation(
                    wo_b[:, i * D:(i + 1) * D], wst4[i % 2][:, :], AF.Copy),
                    event=f"wo_cast{i}")

        def p4_scalar(E):
            for t in range(NTT):
                E.wait(f"g4_{t}")
                if t >= 2:
                    E.wait(f"m1_{t - 2}")
                E.do("ACT", lambda e, t=t: e.activation(
                    gsil[t % 2][:, :], gb4[t % 2][:, :], AF.Silu), event=f"sil{t}")
                E.wait(f"comb{t}")
                E.do(f"osp{t % 2}", lambda e, t=t: e.dma_start(
                    out=io["out"][t * 128:(t + 1) * 128, :], in_=ob4[t % 2][:, :]),
                    16, event=f"osp{t}")

        def p4_tensor(E):
            E.wait("wo_cast7", "exp7_143", "exp7_142", "rbscp7_7", "otbm6", "otbm7")
            for t in range(NTT):
                if t >= 2:
                    E.wait(f"m1_{t - 2}")
                for oc in range(2):
                    for p in range(8):
                        E.do("PE", lambda e, t=t, oc=oc, p=p: e.matmul(
                            pS4[:, (t % 2) * 1024 + oc * 512:(t % 2) * 1024 + (oc + 1) * 512],
                            OT[:, p * T + t * 128:p * T + (t + 1) * 128],
                            wo_b[:, p * D + oc * 512:p * D + (oc + 1) * 512],
                            start=(p == 0), stop=(p == 7)),
                            inc=1 if p == 7 else 0,
                            event=f"womm{t}_{oc}" if p == 7 else None)

        def p4_vec(E):
            for t in range(NTT):
                E.wait(f"womm{t}_0", f"womm{t}_1", f"sil{t}")
                E.do("DVE", lambda e, t=t: e.tensor_mul(
                    tmp4[:, :], pS4[:, (t % 2) * 1024:(t % 2) * 1024 + 1024],
                    gsil[t % 2][:, :]), event=f"m1_{t}")
                E.wait(f"g4_{t}", f"m1_{t}")
                if t >= 2:
                    E.wait(f"osp{t - 2}")
                E.do("DVE", lambda e, t=t: e.tensor_add(
                    ob4[t % 2][:, :], tmp4[:, :], xb4[t % 2][:, :]), event=f"comb{t}")

        def m_sync(E):
            p4_sync_wo(E)
            p3_sync(E)
            p4_sync(E)

        def m_tensor(E):
            p3_tensor(E)
            p4_tensor(E)

        def m_vector(E):
            p3_vec(E)
            p4_vec(E)

        def m_scalar(E):
            p4_scalar_casts(E)
            p3_scalar(E)
            p4_scalar(E)

        run_phase({"sync": m_sync, "tensor": m_tensor, "vector": m_vector,
                   "scalar": m_scalar, "gpsimd": p3_gp}, post_dry=p3_post_dry, name="P34")
        p4.close()
        p3.close()
        p34.close()

    return nc, ctx


def kernel(**inputs):
    in_maps = host_inputs(inputs)
    nc, _ = build()
    res = run_bass_kernel_spmd(nc, in_maps, core_ids=list(range(8)))
    return np.stack([np.asarray(res.results[c]["out"]) for c in range(8)], axis=0)
